# version 17
# baseline (speedup 1.0000x reference)
import numpy as np
import concourse.bass as bass
import concourse.mybir as mybir
from concourse.bass_utils import run_bass_kernel_spmd
from concourse.alu_op_type import AluOpType as OP
from contextlib import ExitStack

F32 = mybir.dt.float32
BF16 = mybir.dt.bfloat16
U8 = mybir.dt.uint8
AF = mybir.ActivationFunctionType
AX = mybir.AxisListType.X

NCORES = 8
D = 2048
NG = 16
TM = 1024
WW = 256
TA = TM + WW
DM = 1024
NMG = 8
NSH = 2
NRT = 16
NEXP = NSH + NRT
NCH0 = TA // 128
NCHM = TM // 128
BIG = 1.0e30
CAP = 256
NST = CAP // 128
RBROWS = TM + 128
DEBUG_DUMPS = False

ARENA = 99328
OFF_WA, OFF_WB = 0, 32768
OFF_MID, OFF_GT = 65536, 73728
OFF_XG, OFF_XGT, OFF_DO = 81920, 90112, 94208
OFF_MZ, OFF_MN = 0, 8192
OFF_KA, OFF_KZ = 16384, 21504
OFF_S1, OFF_S2, OFF_SH = 26624, 31744, 36864
OFF_XT, OFF_OT = 41984, 58368
OFF_XNF, OFF_SQD = 74752, 91136


class Emitter:
    ENGS = ("tensor", "vector", "scalar", "sync", "gpsimd")

    SEMS = ("m_t", "m_v", "m_s", "m_qx", "m_qo", "m_qc", "m_qa", "m_qb",
            "m_qz", "m_qn", "m_qd", "m_qy", "m_qp", "m_qg",
            "m_qh", "m_qt", "m_qe", "m_qs", "m_qr", "m_qu", "m_qv", "m_qw",
            "m_qi", "m_qj")

    def __init__(self):
        self.ops = {k: [] for k in self.ENGS}
        self.cnt = {s: 0 for s in self.SEMS}
        self.waited = {k: {} for k in self.ENGS}

    def wait(self, eng, sem, val):
        if val is None or val <= 0:
            return
        if self.waited[eng].get(sem, 0) >= val:
            return
        self.waited[eng][sem] = val

        def f(e, sems, _s=sem, _v=val):
            e.wait_ge(sems[_s], _v)

        self.ops[eng].append(f)

    def step(self, eng, sem, fn):
        inc = 16 if sem.startswith("m_q") else 1
        self.cnt[sem] += inc
        v = self.cnt[sem]

        def f(e, sems, _fn=fn, _s=sem, _i=inc):
            _fn(e).then_inc(sems[_s], _i)

        self.ops[eng].append(f)
        return v

    def dma(self, eng, sem, fn):
        if self.cnt[sem] > 0:
            self.wait(eng, sem, self.cnt[sem])
        return self.step(eng, sem, fn)

    def plain(self, eng, fn):
        def f(e, sems, _fn=fn):
            _fn(e)

        self.ops[eng].append(f)


def _make_arena(nc, stack, specs):
    backing = stack.enter_context(nc.sbuf_tensor("s_arena", [128, ARENA], U8))
    base = nc.lookup_mloc(backing).addr
    handles = {}
    for name, shape, dtype, off in specs:
        handles[name] = nc.alloc_sbuf_tensor_at(name, shape, dtype, offset=base + off)
    return handles


def build_nc():
    nc = bass.Bass("TRN2", target_bir_lowering=False, detect_race_conditions=False)

    h_x = nc.dram_tensor("h_x", [TA, D], F32, kind="ExternalInput").ap()
    h_up = nc.dram_tensor("h_up", [NEXP, 128, NG, DM], F32, kind="ExternalInput").ap()
    h_dn = nc.dram_tensor("h_dn", [NEXP, 128, NMG, D], F32, kind="ExternalInput").ap()
    h_wz = nc.dram_tensor("h_wz", [NG, 128, NG, 128], F32, kind="ExternalInput").ap()
    h_wn = nc.dram_tensor("h_wn", [NG, 128, NG, 128], F32, kind="ExternalInput").ap()
    h_wr = nc.dram_tensor("h_wr", [128, NG, 16], F32, kind="ExternalInput").ap()
    h_bz = nc.dram_tensor("h_bz", [128, NG], F32, kind="ExternalInput").ap()
    h_bn = nc.dram_tensor("h_bn", [128, NG], F32, kind="ExternalInput").ap()
    h_eyef = nc.dram_tensor("h_eyef", [128, 128], F32, kind="ExternalInput").ap()
    h_flag = nc.dram_tensor("h_flag", [128, 1], F32, kind="ExternalInput").ap()
    h_cst = nc.dram_tensor("h_cst", [128, 32], F32, kind="ExternalInput").ap()
    h_out = nc.dram_tensor("h_out", [TM, D], F32, kind="ExternalOutput").ap()
    h_xnd = nc.dram_tensor("h_xnd", [TM, D], BF16, kind="Internal").ap()
    h_rb = nc.dram_tensor("h_rb", [2 * RBROWS, D // 2], F32, kind="Internal").ap()
    h_it = nc.dram_tensor("h_it", [NRT * CAP, 1], F32, kind="Internal").ap()
    h_gt2 = nc.dram_tensor("h_gt2", [NRT * CAP, 1], F32, kind="Internal").ap()
    if DEBUG_DUMPS:
        h_d_xn = nc.dram_tensor("h_d_xn", [128, NG * TA], BF16, kind="ExternalOutput").ap()
        h_d_gate = nc.dram_tensor("h_d_gate", [128, NCHM * 16], F32, kind="ExternalOutput").ap()
        h_d_acc1 = nc.dram_tensor("h_d_acc1", [128, NG * TM], F32, kind="ExternalOutput").ap()
        h_d_sg = nc.dram_tensor("h_d_sg", [128, NG * TM], F32, kind="ExternalOutput").ap()
        h_d_lg = nc.dram_tensor("h_d_lg", [128, NCHM * 16], F32, kind="ExternalOutput").ap()

    em = Emitter()

    with ExitStack() as stk:
        e_ = stk.enter_context
        s_xn = e_(nc.sbuf_tensor("s_xn", [128, NG * TA], BF16))
        s_acc = e_(nc.sbuf_tensor("s_acc", [128, NG * TM], F32))
        s_wr = e_(nc.sbuf_tensor("s_wr", [128, NG * 16], BF16))
        s_gate = e_(nc.sbuf_tensor("s_gate", [128, NCHM * 16], F32))
        s_rep = e_(nc.sbuf_tensor("s_rep", [128, 512], F32))
        s_ones = e_(nc.sbuf_tensor("s_ones", [128, 128], F32))
        s_lg = e_(nc.sbuf_tensor("s_lg", [128, 16], F32))
        s_pr = e_(nc.sbuf_tensor("s_pr", [128, 16], F32))
        s_ms = e_(nc.sbuf_tensor("s_ms", [128, 16], F32))
        s_stat = e_(nc.sbuf_tensor("s_stat", [128, 64], F32))
        s_bz = e_(nc.sbuf_tensor("s_bz", [128, NG], F32))
        s_nbz = e_(nc.sbuf_tensor("s_nbz", [128, NG], F32))
        s_bn = e_(nc.sbuf_tensor("s_bn", [128, NG], F32))
        s_bnp5 = e_(nc.sbuf_tensor("s_bnp5", [128, NG], F32))
        s_eyef = e_(nc.sbuf_tensor("s_eyef", [128, 128], F32))
        s_eyeb = e_(nc.sbuf_tensor("s_eyeb", [128, 128], BF16))
        s_flag = e_(nc.sbuf_tensor("s_flag", [128, 1], F32))
        s_i = e_(nc.sbuf_tensor("s_i", [128, 1], F32))
        s_eps = e_(nc.sbuf_tensor("s_eps", [128, 1], F32))
        s_lgall = e_(nc.sbuf_tensor("s_lgall", [128, NCHM * 16], F32))
        s_cst = e_(nc.sbuf_tensor("s_cst", [128, 32], F32))
        s_it = e_(nc.sbuf_tensor("s_it", [128, 32], F32))
        s_gv = e_(nc.sbuf_tensor("s_gv", [128, 32], F32))
        s_rkt = e_(nc.sbuf_tensor("s_rkt", [128, 128], F32))
        s_ti = e_(nc.sbuf_tensor("s_ti", [128, 16], F32))
        s_u32 = e_(nc.sbuf_tensor("s_u32", [128, 32], mybir.dt.uint32))

        ar = _make_arena(nc, stk, [
            ("s_wa", [128, NG * DM], BF16, OFF_WA),
            ("s_wb", [128, NMG * D], BF16, OFF_WB),
            ("s_mid", [128, NMG * CAP], F32, OFF_MID),
            ("s_gt", [128, NMG * 512], BF16, OFF_GT),
            ("s_xg", [128, NG * CAP], BF16, OFF_XG),
            ("s_xgt", [128, D], BF16, OFF_XGT),
            ("s_gfm", [128, 1024], F32, OFF_XGT),
            ("s_do", [128, 1024], F32, OFF_DO),
            ("s_mz", [128, 2 * 2048], BF16, OFF_MZ),
            ("s_mn", [128, 2 * 2048], BF16, OFF_MN),
            ("s_ka", [128, TA], F32, OFF_KA),
            ("s_kz", [128, TA], F32, OFF_KZ),
            ("s_s1", [128, TA], F32, OFF_S1),
            ("s_s2", [128, TA], F32, OFF_S2),
            ("s_sh", [128, TA], F32, OFF_SH),
            ("s_xt", [128, 2 * D], F32, OFF_XT),
            ("s_ot", [128, 2 * D], F32, OFF_OT),
            ("s_xnf", [128, 2 * D], F32, OFF_XNF),
            ("s_xnf3", [128, D], F32, OFF_SQD),
        ])
        s_wa, s_wb, s_mid, s_gt = ar["s_wa"], ar["s_wb"], ar["s_mid"], ar["s_gt"]
        s_xg, s_xgt, s_gfm, s_do = ar["s_xg"], ar["s_xgt"], ar["s_gfm"], ar["s_do"]

        def xgt_ap(st):
            return s_xgt[:, 0:D] if st == 0 else s_gt[:, NMG * CAP:NMG * 512]
        s_mz, s_mn = ar["s_mz"], ar["s_mn"]
        s_ka, s_kz, s_s1, s_s2, s_sh = ar["s_ka"], ar["s_kz"], ar["s_s1"], ar["s_s2"], ar["s_sh"]
        s_xt, s_ot, s_xnf = ar["s_xt"], ar["s_ot"], ar["s_xnf"]
        s_xnf3 = ar["s_xnf3"]

        def xnf_ap(i):
            sl = i % 3
            if sl == 2:
                return s_xnf3[:, 0:D]
            return s_xnf[:, sl * D:(sl + 1) * D]

        def xt0_ap(i):
            sl = i % 3
            if sl == 2:
                return s_ot[:, 0:D]
            return s_xt[:, sl * D:(sl + 1) * D]

        ps = e_(nc.psum_tensor("ps_pool", [128, 4096], F32))

        sems = {n: e_(nc.semaphore(n)) for n in Emitter.SEMS}

        xnr = s_xn[:].rearrange("p (g t) -> p g t", t=TA)

        bank_free = {}
        v_norm, s_sq, v_ve, v_vv, s_sqrt = {}, {}, {}, {}, {}
        t_pack, v_evac = {}, {}

        for fn in (lambda g: g.dma_start(s_wr[:], h_wr),
                   lambda g: g.dma_start(s_bz[:], h_bz),
                   lambda g: g.dma_start(s_bn[:], h_bn),
                   lambda g: g.dma_start(s_eyef[:], h_eyef),
                   lambda g: g.dma_start(s_flag[:], h_flag),
                   lambda g: g.dma_start(s_cst[:], h_cst),
                   lambda g: g.dma_start(s_eyeb[:], h_eyef)):
            qc_all = em.step("gpsimd", "m_qc", fn)
        qw_up = {}

        em.wait("vector", "m_qc", qc_all)
        em.plain("vector", lambda v: v.memset(s_ones[:], 1.0))
        em.plain("vector", lambda v: v.memset(s_eps[:], 1.0e-5))
        em.plain("vector", lambda v: v.tensor_scalar_mul(s_nbz[:], s_bz[:], -1.0))
        v_const = em.step("vector", "m_v",
                          lambda v: v.tensor_scalar_add(s_bnp5[:], s_bn[:], 0.5))
        em.wait("tensor", "m_qc", qc_all)
        em.wait("scalar", "m_qc", qc_all)
        em.wait("scalar", "m_v", v_const)

        order0 = list(range(2, NCH0)) + [0, 1]
        qx0, qsem0, wb_by_idx = {}, {}, {}
        for idx, i in enumerate(order0):
            eng0, sem0 = (("sync", "m_qx") if idx % 2 == 0 else ("scalar", "m_qy"))
            qsem0[i] = sem0
            if idx >= 3:
                em.wait(eng0, "m_v", v_norm[order0[idx - 3]])
            qx0[i] = em.dma(eng0, sem0, lambda s, _i=i, _x=idx: s.dma_start(
                xt0_ap(_x), h_x[_i * 128:(_i + 1) * 128, :]))

            if idx == 6:
                em.wait("gpsimd", "m_qx", qx0[order0[4]])
                em.wait("gpsimd", "m_qy", qx0[order0[5]])
                qw_up[0] = em.dma("gpsimd", "m_qa", lambda g: g.dma_start(
                    s_wa[:], h_up[0], max_dma_last_dim=8192))

            xt = xt0_ap(idx)
            B = 32 * (idx % 2)
            em.wait("vector", qsem0[i], qx0[i])
            for q in range(4):
                fn = (lambda v, _xt=xt, _B=B, _q=q: v.bn_stats(
                    s_stat[:, _B + 6 * _q:_B + 6 * (_q + 1)],
                    _xt[:, 512 * _q:512 * (_q + 1)]))
                if q == 3:
                    v_bs = em.step("vector", "m_v", fn)
                else:
                    em.plain("vector", fn)
            em.wait("vector", "m_v", v_bs)
            v_ag = em.step("vector", "m_v", lambda v, _B=B: v.bn_aggr(
                s_stat[:, _B + 24:_B + 26], s_stat[:, _B:_B + 24]))
            em.wait("scalar", "m_v", v_ag)
            s_sqrt[i] = em.step("scalar", "m_s", lambda s, _B=B: s.activation(
                s_stat[:, _B + 26:_B + 27], s_stat[:, _B + 25:_B + 26], AF.Sqrt,
                bias=s_eps[:, 0:1]))
            em.wait("vector", "m_s", s_sqrt[i])
            v_rstd = em.step("vector", "m_v", lambda v, _B=B: v.reciprocal(
                s_stat[:, _B + 27:_B + 28], s_stat[:, _B + 26:_B + 27]))
            em.wait("vector", "m_v", v_rstd)
            if idx >= 3:
                em.wait("vector", "m_t", t_pack[(order0[idx - 3], 3)])
            if idx >= 3 and wb_by_idx.get(idx - 3) is not None:
                em.wait("vector", "m_qh", wb_by_idx[idx - 3])
            v_norm[i] = em.step("vector", "m_v", lambda v, _xt=xt, _x=idx, _B=B: v.tensor_scalar(
                xnf_ap(_x), _xt, s_stat[:, _B + 24:_B + 25], s_stat[:, _B + 27:_B + 28],
                op0=OP.subtract, op1=OP.mult))
            if i >= 2:
                em.wait("gpsimd", "m_v", v_norm[i])
                wb_by_idx[idx] = em.dma("gpsimd", "m_qh", lambda g, _i=i, _x=idx: g.dma_start(
                    h_xnd[(_i - 2) * 128:(_i - 1) * 128, :], xnf_ap(_x)))
            def _emit_evacs(_i):
                for p in range(4):
                    b2 = p
                    em.wait("scalar", "m_t", t_pack[(_i, p)])
                    v_evac[(_i, p)] = em.step("scalar", "m_s",
                                              lambda s, _b=b2, _p=p, _i2=_i: s.activation(
                        xnr[:, 4 * _p:4 * _p + 4, _i2 * 128:(_i2 + 1) * 128],
                        ps[:, _b * 512:_b * 512 + 512].rearrange("p (g t) -> p g t", t=128),
                        AF.Copy))
                    bank_free[b2] = ("m_s", v_evac[(_i, p)])
            if idx >= 1:
                _emit_evacs(order0[idx - 1])
            for p in range(4):
                b = p
                em.wait("tensor", "m_v", v_norm[i])
                bf = bank_free.get(b)
                if bf:
                    em.wait("tensor", bf[0], bf[1])
                for q in range(4):
                    g = 4 * p + q
                    fn = lambda t, _b=b, _q=q, _x=idx, _g=g: t.transpose(
                        ps[:, _b * 512 + _q * 128: _b * 512 + (_q + 1) * 128],
                        xnf_ap(_x)[:, _g * 128:(_g + 1) * 128], s_eyef[:])
                    if q == 3:
                        t_pack[(i, p)] = em.step("tensor", "m_t", fn)
                    else:
                        em.plain("tensor", fn)
        _emit_evacs(order0[-1])
        v_xn_all = v_evac[(order0[-1], 3)]
        if DEBUG_DUMPS:
            em.wait("sync", "m_v", v_xn_all)
            em.dma("sync", "m_qd", lambda s: s.dma_start(h_d_xn, s_xn[:]))

        v_gate_done = {}

        def emit_router():
            em.wait("tensor", "m_qc", qc_all)
            for j in range(NCHM):
                em.wait("tensor", "m_s", v_evac[(2 + j, 3)])
                bf = bank_free.get(7)
                if bf:
                    em.wait("tensor", bf[0], bf[1])
                for k in range(NG):
                    fn = lambda t, _k=k, _j=j: t.matmul(
                        ps[:, 3584 + _j * 16:3600 + _j * 16],
                        s_xn[:, _k * TA + WW + _j * 128:_k * TA + WW + (_j + 1) * 128],
                        s_wr[:, _k * 16:(_k + 1) * 16],
                        start=(_k == 0 and _j == 0), stop=(_k == NG - 1),
                        skip_group_check=(_j > 0))
                    if k == NG - 1:
                        t_rt = em.step("tensor", "m_t", fn)
                    else:
                        em.plain("tensor", fn)
            em.wait("vector", "m_t", t_rt)
            v_lg = em.step("vector", "m_v", lambda v: v.tensor_copy(
                s_lgall[:], ps[:, 3584:3584 + NCHM * 16]))
            bank_free[7] = ("m_v", v_lg)
            prev_m2 = None
            for j in range(NCHM):
                em.wait("scalar", "m_v", v_lg)
                if prev_m2 is not None:
                    em.wait("scalar", "m_v", prev_m2)
                s_exp = em.step("scalar", "m_s", lambda s, _j=j: s.activation(
                    s_pr[:], s_lgall[:, _j * 16:(_j + 1) * 16], AF.Exp))
                em.wait("vector", "m_s", s_exp)
                v_su = em.step("vector", "m_v", lambda v: v.reduce_sum(s_stat[:, 1:2], s_pr[:], axis=AX))
                em.wait("vector", "m_v", v_su)
                v_rc = em.step("vector", "m_v", lambda v: v.reciprocal(s_stat[:, 2:3], s_stat[:, 1:2]))
                em.wait("vector", "m_v", v_rc)
                v_pb = em.step("vector", "m_v", lambda v: v.tensor_scalar_mul(s_ms[:], s_pr[:], s_stat[:, 2:3]))
                em.wait("vector", "m_v", v_pb)
                v_m1 = em.step("vector", "m_v", lambda v: v.reduce_max(s_stat[:, 3:4], s_ms[:], axis=AX))
                em.wait("vector", "m_v", v_m1)
                v_eq = em.step("vector", "m_v", lambda v: v.tensor_single_scalar(
                    s_lg[:], s_ms[:], s_stat[:, 3:4], op=OP.is_ge))
                em.wait("vector", "m_v", v_eq)
                v_mk = em.step("vector", "m_v", lambda v: v.scalar_tensor_tensor(
                    s_pr[:], s_lg[:], -BIG, s_ms[:], op0=OP.mult, op1=OP.add))
                em.wait("vector", "m_v", v_mk)
                v_m2 = em.step("vector", "m_v", lambda v: v.reduce_max(s_stat[:, 4:5], s_pr[:], axis=AX))
                em.wait("vector", "m_v", v_m2)
                prev_m2 = v_m2
                v_kp = em.step("vector", "m_v", lambda v: v.tensor_single_scalar(
                    s_lg[:], s_ms[:], s_stat[:, 4:5], op=OP.is_ge))
                em.wait("vector", "m_v", v_kp)
                v_gate_done[j] = em.step("vector", "m_v", lambda v, _j=j: v.tensor_mul(
                    s_gate[:, _j * 16:(_j + 1) * 16], s_ms[:], s_lg[:]))




        qw_dn = {}
        t_up_last, t_dn_last = {}, {}
        prev_tile_tdn = None
        prev_gate_v = None
        dn_idx = 0
        v_rep_by_r, t_G_by_r, v_gate_last_by_r = {}, {}, {}
        qt_loaded = None
        qs_last = None
        qe_cnt = {}
        t_xg_last = {}
        s_xg_evac = {}

        def vstep(fn):
            vv = em.step("vector", "m_v", fn)
            em.wait("vector", "m_v", vv)
            return vv

        em.wait("vector", "m_t", t_pack[(order0[-1], 3)])
        v_z = em.step("vector", "m_v", lambda v: v.memset(s_do[:], 0.0))
        em.wait("sync", "m_v", v_z)
        qr = 0
        for rrow in range(0, 2 * RBROWS, 128):
            nr = min(128, 2 * RBROWS - rrow)
            qr = em.dma("sync", "m_qr", lambda s, _r=rrow, _n=nr: s.dma_start(
                h_rb[_r:_r + _n, :], s_do[:, 0:_n * 8]))
        qr_zero = qr

        def emit_tables():
            nonlocal qt_loaded, qr_zero
            vz2 = em.step("vector", "m_v", lambda v: v.memset(s_it[:], 0.0))
            em.wait("sync", "m_v", vz2)
            em.dma("sync", "m_qr", lambda s: s.dma_start(h_it[:, 0:1].rearrange(
                "(p a) o -> p (a o)", p=128), s_it[:]))
            qr_zero = em.dma("sync", "m_qr", lambda s: s.dma_start(
                h_gt2[:, 0:1].rearrange("(p a) o -> p (a o)", p=128), s_it[:]))
            em.wait("vector", "m_v", v_gate_done[NCHM - 1])
            for half in range(2):
                for q in range(4):
                    c = half * 4 + q
                    em.wait("tensor", "m_v", v_gate_done[c])
                    bf = bank_free.get(6)
                    if bf:
                        em.wait("tensor", bf[0], bf[1])
                    fn = lambda t, _q=q, _c=c: t.transpose(
                        ps[0:16, 3072 + _q * 128:3072 + (_q + 1) * 128],
                        s_gate[:, _c * 16:(_c + 1) * 16], s_eyef[:])
                    if q == 3:
                        t_gx = em.step("tensor", "m_t", fn)
                    else:
                        em.plain("tensor", fn)
                em.wait("scalar", "m_t", t_gx)
                sv = em.step("scalar", "m_s", lambda s, _h=half: s.activation(
                    s_gfm[0:16, _h * 512:(_h + 1) * 512], ps[0:16, 3072:3584], AF.Copy))
                bank_free[6] = ("m_s", sv)
            em.wait("vector", "m_s", sv)
            em.wait("vector", "m_qr", qr_zero)
            vstep(lambda v: v.tensor_single_scalar(
                s_do[0:16, 0:1024], s_gfm[0:16, 0:1024], 0.0, op=OP.is_gt))
            v_rk = vstep(lambda v: v.tensor_tensor_scan(
                s_gfm[0:16, 0:1024], s_do[0:16, 0:1024], s_do[0:16, 0:1024], 0.0,
                op0=OP.add, op1=OP.bypass))
            for c in range(8):
                em.wait("tensor", "m_v", v_rk)
                bf = bank_free.get(6)
                if bf:
                    em.wait("tensor", bf[0], bf[1])
                fn = lambda t, _c=c: t.transpose(
                    ps[:, 3072 + _c * 16:3072 + (_c + 1) * 16],
                    s_gfm[0:16, _c * 128:(_c + 1) * 128], s_eyef[0:16, 0:16])
                if c == 7:
                    t_rk = em.step("tensor", "m_t", fn)
                else:
                    em.plain("tensor", fn)
            em.wait("scalar", "m_t", t_rk)
            s_rv = em.step("scalar", "m_s", lambda s: s.activation(
                s_rkt[:, 0:128], ps[:, 3072:3200], AF.Copy))
            bank_free[6] = ("m_s", s_rv)
            em.wait("vector", "m_s", s_rv)
            qt = 0
            for c in range(NCHM):
                gate_c = s_gate[:, c * 16:(c + 1) * 16]
                rk_c = s_rkt[:, c * 16:(c + 1) * 16]
                if c >= 2:
                    for sname, qv in qt_by_c[c - 2]:
                        em.wait("vector", sname, qv)
                vstep(lambda v, _g=gate_c: v.tensor_single_scalar(
                    s_stat[:, 0:16], _g, 0.0, op=OP.is_gt))
                vstep(lambda v: v.tensor_tensor_scan(
                    s_stat[:, 16:32], s_stat[:, 0:16], s_stat[:, 0:16], 0.0,
                    op0=OP.add, op1=OP.bypass))
                vstep(lambda v: v.tensor_single_scalar(
                    s_lg[:], s_stat[:, 16:32], 1.0, op=OP.is_equal))
                vstep(lambda v: v.tensor_mul(s_ms[:], s_lg[:], s_stat[:, 0:16]))
                vstep(lambda v: v.tensor_single_scalar(
                    s_lg[:], s_stat[:, 16:32], 2.0, op=OP.is_equal))
                vstep(lambda v: v.tensor_mul(s_pr[:], s_lg[:], s_stat[:, 0:16]))
                vstep(lambda v, _r=rk_c: v.tensor_single_scalar(
                    s_stat[:, 32:48], _r, 129.0, op=OP.is_ge))
                vstep(lambda v, _r=rk_c: v.tensor_scalar_add(s_stat[:, 48:64], _r, -1.0))
                vstep(lambda v: v.scalar_tensor_tensor(
                    s_stat[:, 48:64], s_stat[:, 32:48], -128.0, s_stat[:, 48:64],
                    op0=OP.mult, op1=OP.add))
                vstep(lambda v: v.scalar_tensor_tensor(
                    s_stat[:, 0:16], s_stat[:, 48:64], 32.0, s_cst[:, 0:16],
                    op0=OP.mult, op1=OP.add))
                vstep(lambda v: v.tensor_add(s_stat[:, 16:32], s_stat[:, 0:16],
                                             s_stat[:, 32:48]))
                B2 = (c % 2) * 8
                vstep(lambda v: v.tensor_mul(s_lg[:], s_ms[:], s_stat[:, 16:32]))
                vstep(lambda v, _B=B2: v.reduce_sum(s_ti[:, _B:_B + 1], s_lg[:], axis=AX))
                vstep(lambda v: v.tensor_mul(s_lg[:], s_pr[:], s_stat[:, 16:32]))
                vstep(lambda v, _B=B2: v.reduce_sum(s_ti[:, _B + 1:_B + 2], s_lg[:], axis=AX))
                vstep(lambda v, _g=gate_c: v.tensor_mul(s_lg[:], s_ms[:], _g))
                vstep(lambda v, _B=B2: v.reduce_sum(s_ti[:, _B + 2:_B + 3], s_lg[:], axis=AX))
                vstep(lambda v, _g=gate_c: v.tensor_mul(s_lg[:], s_pr[:], _g))
                vstep(lambda v, _B=B2: v.reduce_sum(s_ti[:, _B + 3:_B + 4], s_lg[:], axis=AX))
                vstep(lambda v, _B=B2, _c=c: v.tensor_scalar(
                    s_ti[:, _B + 4:_B + 5], s_cst[:, 16:17], 1.0, float(_c * 128 + 1),
                    op0=OP.mult, op1=OP.add))
                cb = (c % 2) * 2
                vstep(lambda v, _B=B2, _cb=cb: v.tensor_copy(
                    s_u32[:, _cb:_cb + 1], s_ti[:, _B:_B + 1]))
                v_cst2 = vstep(lambda v, _B=B2, _cb=cb: v.tensor_copy(
                    s_u32[:, _cb + 1:_cb + 2], s_ti[:, _B + 1:_B + 2]))
                em.wait("gpsimd", "m_v", v_cst2)
                em.wait("gpsimd", "m_qr", qr_zero)
                from concourse.bass import IndirectOffsetOnAxis as IOA
                qt = []
                for si, (srcc, idxc) in enumerate(((4, 0), (4, 1), (2, 0), (3, 1))):
                    tgt = h_it if srcc == 4 else h_gt2
                    sname = "m_qt" if si % 2 == 0 else "m_qu"
                    qt.append((sname, em.dma("gpsimd", sname, lambda g, _s=srcc,
                              _i=idxc, _B=B2, _cb=cb, _t=tgt: g.indirect_dma_start(
                        _t, IOA(ap=s_u32[:, _cb + _i:_cb + _i + 1], axis=0),
                        s_ti[:, _B + _s:_B + _s + 1], None))))
                qt_by_c[c] = qt
            qt = em.dma("gpsimd", "m_qt", lambda g: g.dma_start(
                s_it[:], h_it[:, 0:1].rearrange("(p a) o -> p (a o)", p=128)))
            qt_loaded = em.dma("gpsimd", "m_qt", lambda g: g.dma_start(
                s_gv[:], h_gt2[:, 0:1].rearrange("(p a) o -> p (a o)", p=128)))

        def emit_reps(re):
            if re >= 1:
                em.wait("vector", "m_t", t_G_by_r[re - 1])
            em.wait("vector", "m_qt", qt_loaded)
            for st in range(NST):
                col = re * NST + st
                fn2 = lambda v, _st=st, _c=col: v.tensor_scalar_mul(
                    s_rep[:, _st * 128:(_st + 1) * 128], s_ones[:],
                    s_gv[:, _c:_c + 1])
                if st == NST - 1:
                    v_rep_by_r[re] = em.step("vector", "m_v", fn2)
                else:
                    em.plain("vector", fn2)

        def emit_idxprep(re):
            par = (re % 4) * 8
            tb = 8 + (re % 2) * 4
            em.wait("vector", "m_qt", qt_loaded)
            if re >= 4:
                if (re - 4, NST - 1) in qe_cnt:
                    em.wait("vector", *qe_cnt[(re - 4, NST - 1)])
                for _sp in qs_by_re.get(re - 4, ()):
                    em.wait("vector", *_sp)
            for st in range(NST):
                col = re * NST + st
                ti = s_it[:, col:col + 1]
                vstep(lambda v, _ti=ti, _tb=tb: v.tensor_scalar_add(
                    s_ti[:, _tb:_tb + 1], _ti, -1.0))
                vstep(lambda v, _tb=tb: v.tensor_scalar(
                    s_ti[:, _tb + 1:_tb + 2], s_ti[:, _tb:_tb + 1], 0.0, None,
                    op0=OP.max))
                vstep(lambda v, _p=par, _tb=tb, _st=st: v.tensor_copy(
                    s_u32[:, _p + _st:_p + _st + 1], s_ti[:, _tb + 1:_tb + 2]))
                vstep(lambda v, _ti=ti, _tb=tb: v.tensor_single_scalar(
                    s_ti[:, _tb + 1:_tb + 2], _ti, 0.0, op=OP.is_equal))
                vstep(lambda v, _tb=tb: v.tensor_sub(
                    s_ti[:, _tb + 2:_tb + 3], s_cst[:, 17:18], s_ti[:, _tb:_tb + 1]))
                vstep(lambda v, _tb=tb: v.tensor_mul(
                    s_ti[:, _tb + 2:_tb + 3], s_ti[:, _tb + 1:_tb + 2],
                    s_ti[:, _tb + 2:_tb + 3]))
                vstep(lambda v, _tb=tb: v.tensor_add(
                    s_ti[:, _tb + 3:_tb + 4], s_ti[:, _tb:_tb + 1],
                    s_ti[:, _tb + 2:_tb + 3]))
                vstep(lambda v, _p=par, _tb=tb, _st=st: v.tensor_copy(
                    s_u32[:, _p + 2 + _st:_p + 3 + _st], s_ti[:, _tb + 3:_tb + 4]))
                vstep(lambda v, _tb=tb: v.tensor_scalar_add(
                    s_ti[:, _tb + 3:_tb + 4], s_ti[:, _tb + 3:_tb + 4], float(RBROWS)))
                vstep(lambda v, _p=par, _tb=tb, _st=st: v.tensor_copy(
                    s_u32[:, _p + 4 + _st:_p + 5 + _st], s_ti[:, _tb + 3:_tb + 4]))

        def emit_gather_dma(re):
            from concourse.bass import IndirectOffsetOnAxis as IOA
            par = (re % 4) * 8
            em.wait("gpsimd", "m_v", v_idx_done[re])
            em.wait("gpsimd", "m_qh", wb_all)
            for st in range(NST):
                if re > 0:
                    em.wait("gpsimd", "m_t", t_xgp_by.get((re - 1, st)))
                elif st == 1:
                    em.wait("gpsimd", "m_t", t_dn_last[1])
                sn = "m_qe" if st == 0 else "m_qu"
                qe_cnt[(re, st)] = (sn, em.dma("gpsimd", sn, lambda g, _p=par, _st=st:
                                               g.indirect_dma_start(
                    xgt_ap(_st), None, h_xnd,
                    IOA(ap=s_u32[:, _p + _st:_p + _st + 1], axis=0))))

        def emit_gather_tp(re):
            for st in range(NST):
                for p in range(4):
                    bf = bank_free.get(p)
                    if bf:
                        em.wait("tensor", bf[0], bf[1])
                    em.wait("tensor", *qe_cnt[(re, st)])
                    for q in range(4):
                        g2 = 4 * p + q
                        fn = lambda t, _b=p, _q=q, _g=g2, _st=st: t.matmul(
                            ps[:, _b * 512 + _q * 128:_b * 512 + (_q + 1) * 128],
                            xgt_ap(_st)[:, _g * 128:(_g + 1) * 128],
                            s_eyeb[:], start=True, stop=True)
                        if q == 3:
                            t_xgp = em.step("tensor", "m_t", fn)
                        else:
                            em.plain("tensor", fn)
                    em.wait("scalar", "m_t", t_xgp)
                    em.wait("scalar", "m_t", t_up_last.get(re + NSH - 1))
                    sv2 = em.step("scalar", "m_s", lambda s, _b=p, _p2=p, _st=st: s.activation(
                        s_xg[:].rearrange("p (k s) -> p k s", s=CAP)[
                            :, 4 * _p2:4 * _p2 + 4, _st * 128:(_st + 1) * 128],
                        ps[:, _b * 512:_b * 512 + 512].rearrange("p (g t) -> p g t", t=128),
                        AF.Copy))
                    bank_free[p] = ("m_s", sv2)
                    s_xg_evac[re] = sv2
                t_xgp_by[(re, st)] = t_xgp

        v_idx_done = {}
        t_xgp_by = {}
        qt_by_c = {}
        qs_by_re = {}
        wb_all = max(wb_by_idx.values())

        em.wait("gpsimd", "m_v", v_norm[order0[-1]])
        q0dn = em.dma("gpsimd", "m_qb",
                      lambda g: g.dma_start(s_wb[:], h_dn[0], max_dma_last_dim=8192))
        for _sl in range(4):
            qw_dn[(0, _sl)] = ("m_qb", q0dn)

        for e in range(NSH):
            for tt in range(2):
                if e == 0 and tt == 1:
                    emit_router()
                if e == 1 and tt == 0:
                    emit_tables()
                    emit_reps(0)
                    v_idx_done[0] = None
                    emit_idxprep(0)
                    v_idx_done[0] = em.cnt["m_v"]
                toff = WW + tt * 512
                s_gelu = {}
                em.wait("tensor", "m_s", v_evac[(5 if tt == 0 else NCH0 - 1, 3)])
                em.wait("tensor", "m_qa", qw_up[e])
                for m in range(NMG):
                    b = m % 4
                    bf = bank_free.get(b)
                    if bf:
                        em.wait("tensor", bf[0], bf[1])
                    for k in range(NG):
                        fn = lambda t, _b=b, _m=m, _k=k, _to=toff: t.matmul(
                            ps[:, _b * 512:(_b + 1) * 512],
                            s_wa[:, _k * DM + _m * 128:_k * DM + (_m + 1) * 128],
                            s_xn[:, _k * TA + _to:_k * TA + _to + 512],
                            start=(_k == 0), stop=(_k == NG - 1))
                        if k == NG - 1:
                            t_upg = em.step("tensor", "m_t", fn)
                        else:
                            em.plain("tensor", fn)
                    if m == NMG - 1:
                        t_up_last[e] = t_upg
                    em.wait("scalar", "m_t", t_upg)
                    if prev_tile_tdn is not None:
                        em.wait("scalar", "m_t", prev_tile_tdn)
                    s_gelu[m] = em.step("scalar", "m_s", lambda s, _b=b, _m=m: s.activation(
                        s_gt[:, _m * 512:(_m + 1) * 512],
                        ps[:, _b * 512:(_b + 1) * 512], AF.Gelu))
                    bank_free[b] = ("m_s", s_gelu[m])
                if tt == 1:
                    em.wait("gpsimd", "m_t", t_up_last[e])
                    qw_up[e + 1] = em.dma("gpsimd", "m_qa", lambda g, _e=e + 1:
                                          g.dma_start(s_wa[:], h_up[_e],
                                                      max_dma_last_dim=8192))
                for hg in range(NG):
                    if hg % 4 == 0:
                        em.wait("tensor", *qw_dn[(e, hg // 4)])
                    b = 4 + (dn_idx % 2)
                    dn_idx += 1
                    bf = bank_free.get(b)
                    if bf:
                        em.wait("tensor", bf[0], bf[1])
                    for k in range(NMG):
                        em.wait("tensor", "m_s", s_gelu[k])
                        fn = lambda t, _b=b, _hg=hg, _k=k: t.matmul(
                            ps[:, _b * 512:(_b + 1) * 512],
                            s_wb[:, _k * D + _hg * 128:_k * D + (_hg + 1) * 128],
                            s_gt[:, _k * 512:(_k + 1) * 512],
                            start=(_k == 0), stop=(_k == NMG - 1))
                        if k == NMG - 1:
                            t_dng = em.step("tensor", "m_t", fn)
                        else:
                            em.plain("tensor", fn)
                    em.wait("vector", "m_t", t_dng)
                    aslice = (hg * TM + tt * 512, hg * TM + tt * 512 + 512)
                    if e == 0:
                        v_add = em.step("vector", "m_v", lambda v, _b=b, _a=aslice: v.tensor_copy(
                            s_acc[:, _a[0]:_a[1]], ps[:, _b * 512:(_b + 1) * 512]))
                    else:
                        v_add = em.step("vector", "m_v", lambda v, _b=b, _a=aslice: v.tensor_tensor(
                            s_acc[:, _a[0]:_a[1]], s_acc[:, _a[0]:_a[1]],
                            ps[:, _b * 512:(_b + 1) * 512], op=OP.add))
                    bank_free[b] = ("m_v", v_add)
                    if tt == 1 and hg % 4 == 3:
                        sl = hg // 4
                        sn = "m_qb" if sl % 2 == 0 else "m_qv"
                        em.wait("gpsimd", "m_t", t_dng)
                        qw_dn[(e + 1, sl)] = (sn, em.dma(
                            "gpsimd", sn, lambda g, _e=e + 1, _sl=sl: g.dma_start(
                                s_wb[:].rearrange("p (k d) -> p k d", k=NMG)[
                                    :, :, _sl * 512:(_sl + 1) * 512],
                                h_dn[_e][:, :, _sl * 512:(_sl + 1) * 512])))
                t_dn_last[e] = t_dng
                prev_tile_tdn = t_dng
                v_dn_final = v_add

        prev_qsc = {}
        emit_gather_dma(0)
        emit_gather_tp(0)
        for _q in range(4):
            qw_up[(NSH, _q)] = ("m_qa", qw_up[NSH])

        from concourse.bass import IndirectOffsetOnAxis as IOA2
        for re in range(NRT):
            e = re + NSH
            gb = 6
            par = (re % 4) * 8
            em.wait("tensor", "m_v", v_rep_by_r[re])
            bfg = bank_free.get(gb)
            if bfg:
                em.wait("tensor", bfg[0], bfg[1])
            for st in range(NST):
                fn = lambda t, _st=st, _gb=gb: t.transpose(
                    ps[:, _gb * 512 + _st * 128:_gb * 512 + (_st + 1) * 128],
                    s_rep[:, _st * 128:(_st + 1) * 128], s_eyef[:])
                if st == NST - 1:
                    t_G_by_r[re] = em.step("tensor", "m_t", fn)
                else:
                    em.plain("tensor", fn)
            em.wait("vector", "m_t", t_G_by_r[re])
            v_gcp = em.step("vector", "m_v", lambda v: v.tensor_copy(
                s_rep[:, 256:512], ps[:, gb * 512:gb * 512 + 256]))
            bank_free[gb] = ("m_v", v_gcp)
            if re + 1 < NRT:
                emit_reps(re + 1)
                emit_idxprep(re + 1)
                v_idx_done[re + 1] = em.cnt["m_v"]
                emit_gather_dma(re + 1)

            v_gate = {}
            em.wait("tensor", "m_s", s_xg_evac[re])
            for b8 in range(NMG):
                bf = bank_free.get(b8)
                if bf:
                    em.wait("tensor", bf[0], bf[1])
            t_q = {}
            for k in range(NG):
                em.wait("tensor", *qw_up[(e, k // 4)])
                for m in range(NMG):
                    fn = lambda t, _m=m, _k=k: t.matmul(
                        ps[:, _m * 512:_m * 512 + CAP],
                        s_wa[:, _k * DM + _m * 128:_k * DM + (_m + 1) * 128],
                        s_xg[:, _k * CAP:(_k + 1) * CAP],
                        start=(_k == 0), stop=(_k == NG - 1))
                    if m == NMG - 1:
                        t_q[k] = em.step("tensor", "m_t", fn)
                    else:
                        em.plain("tensor", fn)
                if k % 4 == 3 and e + 1 < NEXP:
                    qq = k // 4
                    sn = "m_qa" if qq % 2 == 0 else "m_qw"
                    em.wait("gpsimd", "m_t", t_q[k])
                    qw_up[(e + 1, qq)] = (sn, em.dma(
                        "gpsimd", sn, lambda g, _e=e + 1, _q=qq: g.dma_start(
                            s_wa[:, _q * 4 * DM:(_q + 1) * 4 * DM],
                            h_up[_e][:, _q * 4:(_q + 1) * 4, :],
                            max_dma_last_dim=8192)))
            t_up_last[e] = t_q[NG - 1]
            for m in range(NMG):
                em.wait("scalar", "m_t", t_q[NG - 1])
                em.wait("scalar", "m_t", prev_tile_tdn)
                if prev_gate_v is not None:
                    em.wait("scalar", "m_v", prev_gate_v)
                if m >= 4 and "j1" in prev_qsc:
                    em.wait("scalar", "m_qi", prev_qsc["j1"])
                if m < 4 and "j3" in prev_qsc:
                    em.wait("scalar", "m_qs", prev_qsc["j3"])
                s_gelu_m = em.step("scalar", "m_s", lambda s, _m=m: s.activation(
                    s_mid[:, _m * CAP:(_m + 1) * CAP],
                    ps[:, _m * 512:_m * 512 + CAP], AF.Gelu))
                bank_free[m] = ("m_s", s_gelu_m)
                em.wait("vector", "m_s", s_gelu_m)
                v_gate[m] = em.step("vector", "m_v", lambda v, _m=m: v.tensor_tensor(
                    s_gt[:, _m * CAP:(_m + 1) * CAP],
                    s_mid[:, _m * CAP:(_m + 1) * CAP],
                    s_rep[:, 256:512], op=OP.mult))
            prev_gate_v = v_gate[NMG - 1]
            v_gate_last_by_r[re] = v_gate[NMG - 1]

            if re + 1 < NRT:
                emit_gather_tp(re + 1)

            PB = [s_mid[:, 0:1024], s_mid[:, 1024:2048], s_do[:, 0:1024],
                  s_mid[:, 0:1024]]
            SM = ["m_qs", "m_qi", "m_qj", "m_qs"]
            qs_j = {}
            for st in range(NST):
                for sl in range(4):
                    jj = st * 2 + sl // 2
                    em.wait("tensor", *qw_dn[(e, sl)])
                    b = 4 + (dn_idx % 2)
                    dn_idx += 1
                    bf = bank_free.get(b)
                    if bf:
                        em.wait("tensor", bf[0], bf[1])
                    for k in range(NMG):
                        em.wait("tensor", "m_v", v_gate[k])
                        fn = lambda t, _b=b, _st=st, _sl=sl, _k=k: t.matmul(
                            ps[:, _b * 512:(_b + 1) * 512],
                            s_gt[:, _k * CAP + _st * 128:_k * CAP + (_st + 1) * 128],
                            s_wb[:, _k * D + _sl * 512:_k * D + (_sl + 1) * 512],
                            start=(_k == 0), stop=(_k == NMG - 1))
                        if k == NMG - 1:
                            t_dng = em.step("tensor", "m_t", fn)
                        else:
                            em.plain("tensor", fn)
                    em.wait("vector", "m_t", t_dng)
                    if sl % 2 == 0:
                        if jj == 0 and "j3" in prev_qsc:
                            em.wait("vector", "m_qs", prev_qsc["j3"])
                        elif jj == 1 and "j1" in prev_qsc:
                            em.wait("vector", "m_qi", prev_qsc["j1"])
                        elif jj == 2 and "j2" in prev_qsc:
                            em.wait("vector", "m_qj", prev_qsc["j2"])
                        elif jj == 3:
                            em.wait("vector", "m_qs", qs_j[0])
                    v_ev = em.step("vector", "m_v", lambda v, _b=b, _sl=sl,
                                   _pb=PB[jj]: v.tensor_copy(
                        _pb[:, (_sl % 2) * 512:((_sl % 2) + 1) * 512],
                        ps[:, _b * 512:(_b + 1) * 512]))
                    bank_free[b] = ("m_v", v_ev)
                    if st == NST - 1 and e + 1 < NEXP:
                        sn = "m_qb" if sl % 2 == 0 else "m_qv"
                        em.wait("gpsimd", "m_t", t_dng)
                        qw_dn[(e + 1, sl)] = (sn, em.dma(
                            "gpsimd", sn, lambda g, _e=e + 1, _sl=sl: g.dma_start(
                                s_wb[:].rearrange("p (k d) -> p k d", k=NMG)[
                                    :, :, _sl * 512:(_sl + 1) * 512],
                                h_dn[_e][:, :, _sl * 512:(_sl + 1) * 512])))
                    if sl % 2 == 1:
                        hh = sl // 2
                        em.wait("gpsimd", "m_v", v_ev)
                        em.wait("gpsimd", "m_qr", qr_zero)
                        ic = par + 2 + st + 2 * hh
                        qs_j[jj] = em.dma("gpsimd", SM[jj], lambda g, _ic=ic,
                                          _pb=PB[jj]: g.indirect_dma_start(
                            h_rb, IOA2(ap=s_u32[:, _ic:_ic + 1], axis=0),
                            _pb, None, compute_op=OP.add, oob_is_err=True))
            prev_qsc = {"j1": qs_j[1], "j2": qs_j[2], "j3": qs_j[3]}
            qs_by_re[re] = [("m_qi", qs_j[1]), ("m_qj", qs_j[2]),
                            ("m_qs", qs_j[3])]
            qs_last = qs_by_re[re]
            t_dn_last[e] = t_dng
            prev_tile_tdn = t_dng

        v_merge = {}
        for j in range(NCHM):
            for _sp in qs_last:
                em.wait("sync", *_sp)
            if j >= 2:
                em.wait("sync", "m_v", v_merge[(j - 2, 3)])
            for hh in range(2):
                qm = em.dma("sync", "m_qr", lambda s, _j=j, _h=hh: s.dma_start(
                    s_ot[:].rearrange("p (s h c) -> p s h c", s=2, h=2)[
                        :, _j % 2, _h, :],
                    h_rb[_h * RBROWS + _j * 128:_h * RBROWS + (_j + 1) * 128, :]))
            for p in range(4):
                b = p
                em.wait("tensor", "m_qr", qm)
                bf = bank_free.get(b)
                if bf:
                    em.wait("tensor", bf[0], bf[1])
                for q in range(4):
                    g2 = 4 * p + q
                    fn = lambda t, _b=b, _q=q, _g=g2, _j=j: t.transpose(
                        ps[:, _b * 512 + _q * 128:_b * 512 + (_q + 1) * 128],
                        s_ot[:, (_j % 2) * D + _g * 128:(_j % 2) * D + (_g + 1) * 128],
                        s_eyef[:])
                    if q == 3:
                        t_mg = em.step("tensor", "m_t", fn)
                    else:
                        em.plain("tensor", fn)
                em.wait("vector", "m_t", t_mg)
                v_merge[(j, p)] = em.step("vector", "m_v", lambda v, _b=b, _p=p, _j=j: v.tensor_tensor(
                    s_acc[:].rearrange("p (g t) -> p g t", t=TM)[:, 4 * _p:4 * _p + 4,
                                                                _j * 128:(_j + 1) * 128],
                    s_acc[:].rearrange("p (g t) -> p g t", t=TM)[:, 4 * _p:4 * _p + 4,
                                                                _j * 128:(_j + 1) * 128],
                    ps[:, _b * 512:_b * 512 + 512].rearrange("p (g t) -> p g t", t=128),
                    op=OP.add))
                bank_free[b] = ("m_v", v_merge[(j, p)])
        v_merge_last = v_merge[(NCHM - 1, 3)]

        if DEBUG_DUMPS:
            em.wait("sync", "m_v", v_gate_done[NCHM - 1])
            em.dma("sync", "m_qd", lambda s: s.dma_start(h_d_gate, s_gate[:]))
            em.dma("sync", "m_qd", lambda s: s.dma_start(h_d_lg, s_lgall[:]))

        d_acc1 = None
        if DEBUG_DUMPS:
            em.wait("sync", "m_v", v_dn_final)
            d_acc1 = em.dma("sync", "m_qd", lambda s: s.dma_start(h_d_acc1, s_acc[:]))

        p4slot = [s_xt[:, 0:D], s_xt[:, D:2 * D], s_xnf[:, 0:D],
                  s_xnf[:, D:2 * D], s_xnf3[:, 0:D],
                  s_xt[:, 0:D], s_xt[:, D:2 * D], s_xnf[:, 0:D]]
        qx4, qsem4 = {}, {}

        qw_mz, qw_mn = {}, {}
        t_k_last, t_h_last = {}, {}
        s_az, s_s12, s_sig = {}, {}, {}
        v_b, v_sg = {}, {}
        tiles3 = [(0, 512, 0), (512, 512, 1), (1024, 256, 2)]

        for g in range(NG):
            if g < 2:
                em.wait("gpsimd", "m_t", t_dn_last[NEXP - 1])
            else:
                em.wait("gpsimd", "m_t", t_k_last[g - 2])
            qw_mz[g] = em.dma("gpsimd", "m_qz", lambda gp, _g=g: gp.dma_start(
                s_mz[:, (_g % 2) * 2048:(_g % 2) * 2048 + 2048], h_wz[_g],
                max_dma_last_dim=8192))
            if g >= 2:
                em.wait("gpsimd", "m_t", t_h_last[g - 2])
            qw_mn[g] = em.dma("gpsimd", "m_qn", lambda gp, _g=g: gp.dma_start(
                s_mn[:, (_g % 2) * 2048:(_g % 2) * 2048 + 2048], h_wn[_g],
                max_dma_last_dim=8192))
            if g == 8:
                for _sp in qs_last:
                    em.wait("gpsimd", *_sp)
                for j4 in range(5):
                    qsem4[j4] = "m_qg"
                    qx4[j4] = em.dma("gpsimd", "m_qg", lambda gp, _j=j4: gp.dma_start(
                        p4slot[_j], h_x[WW + _j * 128:WW + (_j + 1) * 128, :]))

            if g == 0:
                em.wait("tensor", "m_s", v_evac[(order0[-1], 3)])
            em.wait("tensor", "m_qz", qw_mz[g])
            if g >= 1:
                em.wait("tensor", "m_s", s_az[g - 1])
            else:
                for b in range(3):
                    bf = bank_free.get(b)
                    if bf:
                        em.wait("tensor", bf[0], bf[1])
            for off, w, b in tiles3:
                for k in range(NG):
                    fn = lambda t, _b=b, _o=off, _w=w, _k=k, _g=g: t.matmul(
                        ps[:, _b * 512:_b * 512 + _w],
                        s_mz[:, (_g % 2) * 2048 + _k * 128:(_g % 2) * 2048 + (_k + 1) * 128],
                        s_xn[:, _k * TA + _o:_k * TA + _o + _w],
                        start=(_k == 0), stop=(_k == NG - 1))
                    if k == NG - 1:
                        t_kt = em.step("tensor", "m_t", fn)
                    else:
                        em.plain("tensor", fn)
            t_k_last[g] = t_kt
            em.wait("tensor", "m_qn", qw_mn[g])
            if g >= 1:
                em.wait("tensor", "m_s", s_s12[g - 1])
            else:
                for b in range(3, 6):
                    bf = bank_free.get(b)
                    if bf:
                        em.wait("tensor", bf[0], bf[1])
            for off, w, b in tiles3:
                for k in range(NG):
                    fn = lambda t, _b=b, _o=off, _w=w, _k=k, _g=g: t.matmul(
                        ps[:, 1536 + _b * 512:1536 + _b * 512 + _w],
                        s_mn[:, (_g % 2) * 2048 + _k * 128:(_g % 2) * 2048 + (_k + 1) * 128],
                        s_xn[:, _k * TA + _o:_k * TA + _o + _w],
                        start=(_k == 0), stop=(_k == NG - 1))
                    if k == NG - 1:
                        t_ht = em.step("tensor", "m_t", fn)
                    else:
                        em.plain("tensor", fn)
            t_h_last[g] = t_ht

            if g >= 1:
                em.wait("scalar", "m_v", v_sg[g - 1])
            em.wait("scalar", "m_t", t_k_last[g])
            em.step("scalar", "m_s", lambda s, _g=g: s.activation(
                s_ka[:], ps[:, 0:TA], AF.Sigmoid, bias=s_nbz[:, _g:_g + 1], scale=-1.0))
            s_az[g] = em.step("scalar", "m_s", lambda s, _g=g: s.activation(
                s_kz[:], ps[:, 0:TA], AF.Sigmoid, bias=s_bz[:, _g:_g + 1]))
            em.wait("scalar", "m_t", t_h_last[g])
            em.step("scalar", "m_s", lambda s, _g=g: s.activation(
                s_s1[:], ps[:, 1536:1536 + TA], AF.Sigmoid, bias=s_bn[:, _g:_g + 1]))
            s_s12[g] = em.step("scalar", "m_s", lambda s, _g=g: s.activation(
                s_s2[:], ps[:, 1536:1536 + TA], AF.Identity, bias=s_bnp5[:, _g:_g + 1]))

            em.wait("vector", "m_s", s_s12[g])
            em.plain("vector", lambda v: v.tensor_max(s_sh[:], s_s1[:], s_s2[:]))
            v_b[g] = em.step("vector", "m_v", lambda v: v.tensor_mul(s_s1[:], s_kz[:], s_sh[:]))
            v_sc1 = em.step("vector", "m_v", lambda v: v.tensor_tensor_scan(
                s_s2[:, 0:WW], s_ka[:, 0:WW], s_s1[:, 0:WW], 0.5,
                op0=OP.mult, op1=OP.add))
            em.wait("vector", "m_v", v_sc1)
            v_t1 = em.step("vector", "m_v", lambda v: v.tensor_scalar(
                s_stat[:, 0:1], s_s2[:, WW - 1:WW], -1.0, 0.5, op0=OP.mult, op1=OP.add))
            em.wait("vector", "m_v", v_t1)
            v_t2 = em.step("vector", "m_v", lambda v: v.tensor_mul(
                s_stat[:, 1:2], s_stat[:, 0:1], s_flag[:]))
            em.wait("vector", "m_v", v_t2)
            v_ini = em.step("vector", "m_v", lambda v: v.tensor_add(
                s_i[:], s_s2[:, WW - 1:WW], s_stat[:, 1:2]))
            em.wait("vector", "m_v", v_ini)
            v_scan2 = em.step("vector", "m_v", lambda v: v.tensor_tensor_scan(
                s_s2[:, WW:TA], s_ka[:, WW:TA], s_s1[:, WW:TA], s_i[:, 0:1],
                op0=OP.mult, op1=OP.add))
            em.wait("scalar", "m_v", v_b[g])
            em.wait("scalar", "m_v", v_merge_last)
            s_sig[g] = em.step("scalar", "m_s", lambda s, _g=g: s.activation(
                s_kz[:, 0:TM], s_acc[:, _g * TM:(_g + 1) * TM], AF.Sigmoid))
            em.wait("vector", "m_s", s_sig[g])
            em.plain("vector", lambda v, _g=g: v.tensor_mul(
                s_ka[:, 0:TM], s_acc[:, _g * TM:(_g + 1) * TM], s_kz[:, 0:TM]))
            if DEBUG_DUMPS and g == 0:
                em.wait("vector", "m_qd", d_acc1)
            v_sg[g] = em.step("vector", "m_v", lambda v, _g=g: v.tensor_mul(
                s_acc[:, _g * TM:(_g + 1) * TM], s_s2[:, WW:TA], s_ka[:, 0:TM]))

        for _b in range(3):
            bank_free[_b] = ("m_s", s_az[NG - 1])
        for _b in range(3, 6):
            bank_free[_b] = ("m_s", s_s12[NG - 1])
        bank_free[6] = ("m_v", v_gate_last_by_r[NRT - 2])
        bank_free[7] = ("m_v", v_gate_last_by_r[NRT - 1])

        if DEBUG_DUMPS:
            em.wait("sync", "m_v", v_sg[NG - 1])
            em.dma("sync", "m_qd", lambda s: s.dma_start(h_d_sg, s_acc[:]))

        v_fadd = {}
        qo, oq = {}, {}
        def emit_p4_load(j):
            eng4, sem4 = (("sync", "m_qx") if j % 2 == 0 else ("scalar", "m_qy"))
            qsem4[j] = sem4
            em.wait(eng4, "m_v", v_fadd[(j - 5, 3)])
            qx4[j] = em.dma(eng4, sem4, lambda s, _j=j: s.dma_start(
                p4slot[_j], h_x[WW + _j * 128:WW + (_j + 1) * 128, :]))
        P4B = (6, 7, 4, 5)
        for j in range(NCHM):
            jb = (j % 2) * D
            for p in range(4):
                b = P4B[p]
                em.wait("tensor", "m_v", v_sg[4 * p + 3])
                bf = bank_free.get(b)
                if bf:
                    em.wait("tensor", bf[0], bf[1])
                for q in range(4):
                    g = 4 * p + q
                    fn = lambda t, _b=b, _q=q, _g=g, _j=j: t.transpose(
                        ps[:, _b * 512 + _q * 128:_b * 512 + (_q + 1) * 128],
                        s_acc[:, _g * TM + _j * 128:_g * TM + (_j + 1) * 128], s_eyef[:])
                    if q == 3:
                        t_p4 = em.step("tensor", "m_t", fn)
                    else:
                        em.plain("tensor", fn)
                em.wait("vector", "m_t", t_p4)
                em.wait("vector", qsem4[j], qx4[j])
                if j >= 2:
                    em.wait("vector", oq[j - 2], qo[j - 2])
                v_fadd[(j, p)] = em.step("vector", "m_v", lambda v, _b=b, _p=p, _j=j, _jb=jb: v.tensor_tensor(
                    s_ot[:, _jb + _p * 512:_jb + (_p + 1) * 512],
                    ps[:, _b * 512:(_b + 1) * 512],
                    p4slot[_j][:, _p * 512:(_p + 1) * 512], op=OP.add))
                bank_free[b] = ("m_v", v_fadd[(j, p)])
            oeng, osem = (("sync", "m_qo") if j % 2 == 0 else ("gpsimd", "m_qp"))
            oq[j] = osem
            em.wait(oeng, "m_v", v_fadd[(j, 3)])
            qo[j] = em.dma(oeng, osem, lambda s, _j=j, _jb=jb: s.dma_start(
                h_out[_j * 128:(_j + 1) * 128, :], s_ot[:, _jb:_jb + D]))
            if j + 5 < NCHM:
                emit_p4_load(j + 5)
        em.wait("sync", "m_qo", qo[NCHM - 2])
        em.wait("sync", "m_qp", qo[NCHM - 1])

        block = stk.enter_context(nc.Block())

        @block.tensor
        def _(t):
            for f in em.ops["tensor"]:
                f(t, sems)

        @block.vector
        def _(v):
            for f in em.ops["vector"]:
                f(v, sems)

        @block.scalar
        def _(s):
            for f in em.ops["scalar"]:
                f(s, sems)

        @block.sync
        def _(s):
            for f in em.ops["sync"]:
                f(s, sems)

        @block.gpsimd
        def _(g):
            for f in em.ops["gpsimd"]:
                f(g, sems)

    return nc


def kernel(L0_attn_x, L0_attn_Wz, L0_attn_bz, L0_attn_Wn, L0_attn_bn,
           L0_ffn_Wsh_up, L0_ffn_Wsh_down, L0_ffn_Wrt_up, L0_ffn_Wrt_down,
           L0_ffn_W_router):
    x = np.ascontiguousarray(np.asarray(L0_attn_x, dtype=np.float32).reshape(8192, D))
    Wz = np.asarray(L0_attn_Wz, dtype=np.float32)
    Wn = np.asarray(L0_attn_Wn, dtype=np.float32)
    bz = np.asarray(L0_attn_bz, dtype=np.float32)
    bn = np.asarray(L0_attn_bn, dtype=np.float32)
    Wr = np.asarray(L0_ffn_W_router, dtype=np.float32)
    up_all = np.concatenate([np.asarray(L0_ffn_Wsh_up, dtype=np.float32),
                             np.asarray(L0_ffn_Wrt_up, dtype=np.float32)], axis=0)
    dn_all = np.concatenate([np.asarray(L0_ffn_Wsh_down, dtype=np.float32),
                             np.asarray(L0_ffn_Wrt_down, dtype=np.float32)], axis=0)

    h_wz = np.ascontiguousarray(Wz.reshape(NG, 128, NG, 128).transpose(0, 3, 2, 1))
    h_wn = np.ascontiguousarray(Wn.reshape(NG, 128, NG, 128).transpose(0, 3, 2, 1))
    h_up = np.ascontiguousarray(up_all.reshape(NEXP, NG, 128, DM).transpose(0, 2, 1, 3))
    h_dn = np.ascontiguousarray(dn_all.reshape(NEXP, NMG, 128, D).transpose(0, 2, 1, 3))
    h_wr = np.ascontiguousarray(Wr.reshape(NG, 128, 16).transpose(1, 0, 2))
    h_bz = np.ascontiguousarray(bz.reshape(NG, 128).T)
    h_bn = np.ascontiguousarray(bn.reshape(NG, 128).T)
    h_eyef = np.eye(128, dtype=np.float32)
    h_cstv = np.zeros((128, 32), dtype=np.float32)
    h_cstv[:, 0:16] = 2.0 * np.arange(16, dtype=np.float32)[None, :]
    h_cstv[:, 16] = np.arange(128, dtype=np.float32)
    h_cstv[:, 17] = np.arange(128, dtype=np.float32) + TM

    nc = build_nc()
    in_maps = []
    for c in range(NCORES):
        if c == 0:
            xs = np.ascontiguousarray(np.concatenate([x[0:WW], x[0:TM]], axis=0))
            flag = np.ones((128, 1), dtype=np.float32)
        else:
            xs = np.ascontiguousarray(x[c * TM - WW:c * TM + TM])
            flag = np.zeros((128, 1), dtype=np.float32)
        in_maps.append({
            "h_x": xs, "h_up": h_up, "h_dn": h_dn, "h_wz": h_wz, "h_wn": h_wn,
            "h_wr": h_wr, "h_bz": h_bz, "h_bn": h_bn, "h_eyef": h_eyef,
            "h_flag": flag, "h_cst": h_cstv,
        })
    res = run_bass_kernel_spmd(nc, in_maps, core_ids=list(range(NCORES)))
    out = np.concatenate([np.asarray(res.results[c]["h_out"]) for c in range(NCORES)],
                         axis=0)
    return out.reshape(1, 8192, D).astype(np.float32, copy=False)


# version 18
# speedup vs baseline: 1.1154x; 1.1154x over previous
import numpy as np
import concourse.bass as bass
import concourse.mybir as mybir
from concourse.bass_utils import run_bass_kernel_spmd
from concourse.alu_op_type import AluOpType as OP
from contextlib import ExitStack

F32 = mybir.dt.float32
BF16 = mybir.dt.bfloat16
U8 = mybir.dt.uint8
AF = mybir.ActivationFunctionType
AX = mybir.AxisListType.X

NCORES = 8
D = 2048
NG = 16
TM = 1024
WW = 256
TA = TM + WW
DM = 1024
NMG = 8
NSH = 2
NRT = 16
NEXP = NSH + NRT
NCH0 = TA // 128
NCHM = TM // 128
BIG = 1.0e30
CAP = 256
NST = CAP // 128
RBROWS = TM + 128
DEBUG_DUMPS = False

ARENA = 99328
OFF_WA, OFF_WB = 0, 32768
OFF_MID, OFF_GT = 65536, 73728
OFF_XG, OFF_XGT, OFF_DO = 81920, 90112, 94208
OFF_MZ, OFF_MN = 0, 8192
OFF_KA, OFF_KZ = 16384, 21504
OFF_S1, OFF_S2, OFF_SH = 26624, 31744, 36864
OFF_XT, OFF_OT = 41984, 58368
OFF_XNF, OFF_SQD = 74752, 91136


class Emitter:
    ENGS = ("tensor", "vector", "scalar", "sync", "gpsimd")

    SEMS = ("m_t", "m_v", "m_s", "m_qx", "m_qo", "m_qc", "m_qa", "m_qb",
            "m_qz", "m_qn", "m_qd", "m_qy", "m_qp", "m_qg",
            "m_qh", "m_qt", "m_qe", "m_qs", "m_qr", "m_qu", "m_qv", "m_qw",
            "m_qi", "m_qj")

    def __init__(self):
        self.ops = {k: [] for k in self.ENGS}
        self.cnt = {s: 0 for s in self.SEMS}
        self.waited = {k: {} for k in self.ENGS}

    def wait(self, eng, sem, val):
        if val is None or val <= 0:
            return
        if self.waited[eng].get(sem, 0) >= val:
            return
        self.waited[eng][sem] = val

        def f(e, sems, _s=sem, _v=val):
            e.wait_ge(sems[_s], _v)

        self.ops[eng].append(f)

    def step(self, eng, sem, fn):
        inc = 16 if sem.startswith("m_q") else 1
        self.cnt[sem] += inc
        v = self.cnt[sem]

        def f(e, sems, _fn=fn, _s=sem, _i=inc):
            _fn(e).then_inc(sems[_s], _i)

        self.ops[eng].append(f)
        return v

    def dma(self, eng, sem, fn):
        if self.cnt[sem] > 0:
            self.wait(eng, sem, self.cnt[sem])
        return self.step(eng, sem, fn)

    def plain(self, eng, fn):
        def f(e, sems, _fn=fn):
            _fn(e)

        self.ops[eng].append(f)


def _make_arena(nc, stack, specs):
    backing = stack.enter_context(nc.sbuf_tensor("s_arena", [128, ARENA], U8))
    base = nc.lookup_mloc(backing).addr
    handles = {}
    for name, shape, dtype, off in specs:
        handles[name] = nc.alloc_sbuf_tensor_at(name, shape, dtype, offset=base + off)
    return handles


def build_nc():
    nc = bass.Bass("TRN2", target_bir_lowering=False, detect_race_conditions=False)

    h_x = nc.dram_tensor("h_x", [TA, D], F32, kind="ExternalInput").ap()
    h_up = nc.dram_tensor("h_up", [NEXP, 128, NG, DM], F32, kind="ExternalInput").ap()
    h_dn = nc.dram_tensor("h_dn", [NEXP, 128, NMG, D], F32, kind="ExternalInput").ap()
    h_wz = nc.dram_tensor("h_wz", [NG, 128, NG, 128], F32, kind="ExternalInput").ap()
    h_wn = nc.dram_tensor("h_wn", [NG, 128, NG, 128], F32, kind="ExternalInput").ap()
    h_wr = nc.dram_tensor("h_wr", [128, NG, 16], F32, kind="ExternalInput").ap()
    h_bz = nc.dram_tensor("h_bz", [128, NG], F32, kind="ExternalInput").ap()
    h_bn = nc.dram_tensor("h_bn", [128, NG], F32, kind="ExternalInput").ap()
    h_eyef = nc.dram_tensor("h_eyef", [128, 128], F32, kind="ExternalInput").ap()
    h_flag = nc.dram_tensor("h_flag", [128, 1], F32, kind="ExternalInput").ap()
    h_cst = nc.dram_tensor("h_cst", [128, 32], F32, kind="ExternalInput").ap()
    h_out = nc.dram_tensor("h_out", [TM, D], F32, kind="ExternalOutput").ap()
    h_xnd = nc.dram_tensor("h_xnd", [TM, D], BF16, kind="Internal").ap()
    h_rb = nc.dram_tensor("h_rb", [2 * RBROWS, D // 2], F32, kind="Internal").ap()
    h_it = nc.dram_tensor("h_it", [NRT * CAP, 1], F32, kind="Internal").ap()
    h_gt2 = nc.dram_tensor("h_gt2", [NRT * CAP, 1], F32, kind="Internal").ap()
    if DEBUG_DUMPS:
        h_d_xn = nc.dram_tensor("h_d_xn", [128, NG * TA], BF16, kind="ExternalOutput").ap()
        h_d_gate = nc.dram_tensor("h_d_gate", [128, NCHM * 16], F32, kind="ExternalOutput").ap()
        h_d_acc1 = nc.dram_tensor("h_d_acc1", [128, NG * TM], F32, kind="ExternalOutput").ap()
        h_d_sg = nc.dram_tensor("h_d_sg", [128, NG * TM], F32, kind="ExternalOutput").ap()
        h_d_lg = nc.dram_tensor("h_d_lg", [128, NCHM * 16], F32, kind="ExternalOutput").ap()

    em = Emitter()

    with ExitStack() as stk:
        e_ = stk.enter_context
        s_xn = e_(nc.sbuf_tensor("s_xn", [128, NG * TA], BF16))
        s_acc = e_(nc.sbuf_tensor("s_acc", [128, NG * TM], F32))
        s_wr = e_(nc.sbuf_tensor("s_wr", [128, NG * 16], BF16))
        s_gate = e_(nc.sbuf_tensor("s_gate", [128, NCHM * 16], F32))
        s_rep = e_(nc.sbuf_tensor("s_rep", [128, 512], F32))
        s_ones = e_(nc.sbuf_tensor("s_ones", [128, 128], F32))
        s_lg = e_(nc.sbuf_tensor("s_lg", [128, 16], F32))
        s_pr = e_(nc.sbuf_tensor("s_pr", [128, 16], F32))
        s_ms = e_(nc.sbuf_tensor("s_ms", [128, 16], F32))
        s_stat = e_(nc.sbuf_tensor("s_stat", [128, 64], F32))
        s_bz = e_(nc.sbuf_tensor("s_bz", [128, NG], F32))
        s_nbz = e_(nc.sbuf_tensor("s_nbz", [128, NG], F32))
        s_bn = e_(nc.sbuf_tensor("s_bn", [128, NG], F32))
        s_bnp5 = e_(nc.sbuf_tensor("s_bnp5", [128, NG], F32))
        s_eyef = e_(nc.sbuf_tensor("s_eyef", [128, 128], F32))
        s_eyeb = e_(nc.sbuf_tensor("s_eyeb", [128, 128], BF16))
        s_flag = e_(nc.sbuf_tensor("s_flag", [128, 1], F32))
        s_i = e_(nc.sbuf_tensor("s_i", [128, 1], F32))
        s_eps = e_(nc.sbuf_tensor("s_eps", [128, 1], F32))
        s_lgall = e_(nc.sbuf_tensor("s_lgall", [128, NCHM * 16], F32))
        s_cst = e_(nc.sbuf_tensor("s_cst", [128, 32], F32))
        s_it = e_(nc.sbuf_tensor("s_it", [128, 32], F32))
        s_gv = e_(nc.sbuf_tensor("s_gv", [128, 32], F32))
        s_rkt = e_(nc.sbuf_tensor("s_rkt", [128, 128], F32))
        s_ti = e_(nc.sbuf_tensor("s_ti", [128, 16], F32))
        s_u32 = e_(nc.sbuf_tensor("s_u32", [128, 32], mybir.dt.uint32))

        ar = _make_arena(nc, stk, [
            ("s_wa", [128, NG * DM], BF16, OFF_WA),
            ("s_wb", [128, NMG * D], BF16, OFF_WB),
            ("s_mid", [128, NMG * CAP], F32, OFF_MID),
            ("s_gt", [128, NMG * 512], BF16, OFF_GT),
            ("s_xg", [128, NG * CAP], BF16, OFF_XG),
            ("s_xgt", [128, D], BF16, OFF_XGT),
            ("s_gfm", [128, 1024], F32, OFF_XGT),
            ("s_do", [128, 1024], F32, OFF_DO),
            ("s_mz", [128, 2 * 2048], BF16, OFF_MZ),
            ("s_mn", [128, 2 * 2048], BF16, OFF_MN),
            ("s_ka", [128, TA], F32, OFF_KA),
            ("s_kz", [128, TA], F32, OFF_KZ),
            ("s_s1", [128, TA], F32, OFF_S1),
            ("s_s2", [128, TA], F32, OFF_S2),
            ("s_sh", [128, TA], F32, OFF_SH),
            ("s_xt", [128, 2 * D], F32, OFF_XT),
            ("s_ot", [128, 2 * D], F32, OFF_OT),
            ("s_xnf", [128, 2 * D], F32, OFF_XNF),
            ("s_xnf3", [128, D], F32, OFF_SQD),
        ])
        s_wa, s_wb, s_mid, s_gt = ar["s_wa"], ar["s_wb"], ar["s_mid"], ar["s_gt"]
        s_xg, s_xgt, s_gfm, s_do = ar["s_xg"], ar["s_xgt"], ar["s_gfm"], ar["s_do"]

        def xgt_ap(st):
            return s_xgt[:, 0:D] if st == 0 else s_gt[:, NMG * CAP:NMG * 512]
        s_mz, s_mn = ar["s_mz"], ar["s_mn"]
        s_ka, s_kz, s_s1, s_s2, s_sh = ar["s_ka"], ar["s_kz"], ar["s_s1"], ar["s_s2"], ar["s_sh"]
        s_xt, s_ot, s_xnf = ar["s_xt"], ar["s_ot"], ar["s_xnf"]
        s_xnf3 = ar["s_xnf3"]

        def xnf_ap(i):
            sl = i % 3
            if sl == 2:
                return s_xnf3[:, 0:D]
            return s_xnf[:, sl * D:(sl + 1) * D]

        def xt0_ap(i):
            sl = i % 3
            if sl == 2:
                return s_ot[:, 0:D]
            return s_xt[:, sl * D:(sl + 1) * D]

        ps = e_(nc.psum_tensor("ps_pool", [128, 4096], F32))

        sems = {n: e_(nc.semaphore(n)) for n in Emitter.SEMS}

        xnr = s_xn[:].rearrange("p (g t) -> p g t", t=TA)

        bank_free = {}
        v_norm, s_sq, v_ve, v_vv, s_sqrt = {}, {}, {}, {}, {}
        t_pack, v_evac = {}, {}

        for fn in (lambda g: g.dma_start(s_wr[:], h_wr),
                   lambda g: g.dma_start(s_bz[:], h_bz),
                   lambda g: g.dma_start(s_bn[:], h_bn),
                   lambda g: g.dma_start(s_eyef[:], h_eyef),
                   lambda g: g.dma_start(s_flag[:], h_flag),
                   lambda g: g.dma_start(s_cst[:], h_cst),
                   lambda g: g.dma_start(s_eyeb[:], h_eyef)):
            qc_all = em.step("gpsimd", "m_qc", fn)
        qw_up = {}

        em.wait("vector", "m_qc", qc_all)
        em.plain("vector", lambda v: v.memset(s_ones[:], 1.0))
        em.plain("vector", lambda v: v.memset(s_eps[:], 1.0e-5))
        em.plain("vector", lambda v: v.tensor_scalar_mul(s_nbz[:], s_bz[:], -1.0))
        v_const = em.step("vector", "m_v",
                          lambda v: v.tensor_scalar_add(s_bnp5[:], s_bn[:], 0.5))
        em.wait("tensor", "m_qc", qc_all)
        em.wait("scalar", "m_qc", qc_all)
        em.wait("scalar", "m_v", v_const)

        order0 = list(range(2, NCH0)) + [0, 1]
        qx0, qsem0, wb_by_idx = {}, {}, {}
        for idx, i in enumerate(order0):
            eng0, sem0 = (("sync", "m_qx") if idx % 2 == 0 else ("scalar", "m_qy"))
            qsem0[i] = sem0
            if idx >= 3:
                em.wait(eng0, "m_v", v_norm[order0[idx - 3]])
            qx0[i] = em.dma(eng0, sem0, lambda s, _i=i, _x=idx: s.dma_start(
                xt0_ap(_x), h_x[_i * 128:(_i + 1) * 128, :]))

            if idx == 6:
                em.wait("gpsimd", "m_qx", qx0[order0[4]])
                em.wait("gpsimd", "m_qy", qx0[order0[5]])
                qw_up[0] = em.dma("gpsimd", "m_qa", lambda g: g.dma_start(
                    s_wa[:], h_up[0], max_dma_last_dim=8192))

            xt = xt0_ap(idx)
            B = 32 * (idx % 2)
            em.wait("vector", qsem0[i], qx0[i])
            for q in range(4):
                fn = (lambda v, _xt=xt, _B=B, _q=q: v.bn_stats(
                    s_stat[:, _B + 6 * _q:_B + 6 * (_q + 1)],
                    _xt[:, 512 * _q:512 * (_q + 1)]))
                if q == 3:
                    v_bs = em.step("vector", "m_v", fn)
                else:
                    em.plain("vector", fn)
            em.wait("vector", "m_v", v_bs)
            v_ag = em.step("vector", "m_v", lambda v, _B=B: v.bn_aggr(
                s_stat[:, _B + 24:_B + 26], s_stat[:, _B:_B + 24]))
            em.wait("scalar", "m_v", v_ag)
            s_sqrt[i] = em.step("scalar", "m_s", lambda s, _B=B: s.activation(
                s_stat[:, _B + 26:_B + 27], s_stat[:, _B + 25:_B + 26], AF.Sqrt,
                bias=s_eps[:, 0:1]))
            em.wait("vector", "m_s", s_sqrt[i])
            v_rstd = em.step("vector", "m_v", lambda v, _B=B: v.reciprocal(
                s_stat[:, _B + 27:_B + 28], s_stat[:, _B + 26:_B + 27]))
            em.wait("vector", "m_v", v_rstd)
            if idx >= 3:
                em.wait("vector", "m_t", t_pack[(order0[idx - 3], 3)])
            if idx >= 3 and wb_by_idx.get(idx - 3) is not None:
                em.wait("vector", "m_qh", wb_by_idx[idx - 3])
            v_norm[i] = em.step("vector", "m_v", lambda v, _xt=xt, _x=idx, _B=B: v.tensor_scalar(
                xnf_ap(_x), _xt, s_stat[:, _B + 24:_B + 25], s_stat[:, _B + 27:_B + 28],
                op0=OP.subtract, op1=OP.mult))
            if i >= 2:
                em.wait("gpsimd", "m_v", v_norm[i])
                wb_by_idx[idx] = em.dma("gpsimd", "m_qh", lambda g, _i=i, _x=idx: g.dma_start(
                    h_xnd[(_i - 2) * 128:(_i - 1) * 128, :], xnf_ap(_x)))
            def _emit_evacs(_i):
                for p in range(4):
                    b2 = p
                    em.wait("scalar", "m_t", t_pack[(_i, p)])
                    v_evac[(_i, p)] = em.step("scalar", "m_s",
                                              lambda s, _b=b2, _p=p, _i2=_i: s.activation(
                        xnr[:, 4 * _p:4 * _p + 4, _i2 * 128:(_i2 + 1) * 128],
                        ps[:, _b * 512:_b * 512 + 512].rearrange("p (g t) -> p g t", t=128),
                        AF.Copy))
                    bank_free[b2] = ("m_s", v_evac[(_i, p)])
            if idx >= 1:
                _emit_evacs(order0[idx - 1])
            for p in range(4):
                b = p
                em.wait("tensor", "m_v", v_norm[i])
                bf = bank_free.get(b)
                if bf:
                    em.wait("tensor", bf[0], bf[1])
                for q in range(4):
                    g = 4 * p + q
                    fn = lambda t, _b=b, _q=q, _x=idx, _g=g: t.transpose(
                        ps[:, _b * 512 + _q * 128: _b * 512 + (_q + 1) * 128],
                        xnf_ap(_x)[:, _g * 128:(_g + 1) * 128], s_eyef[:])
                    if q == 3:
                        t_pack[(i, p)] = em.step("tensor", "m_t", fn)
                    else:
                        em.plain("tensor", fn)
        _emit_evacs(order0[-1])
        v_xn_all = v_evac[(order0[-1], 3)]
        if DEBUG_DUMPS:
            em.wait("sync", "m_v", v_xn_all)
            em.dma("sync", "m_qd", lambda s: s.dma_start(h_d_xn, s_xn[:]))

        v_gate_done = {}

        def emit_router():
            em.wait("tensor", "m_qc", qc_all)
            for j in range(NCHM):
                em.wait("tensor", "m_s", v_evac[(2 + j, 3)])
                bf = bank_free.get(7)
                if bf:
                    em.wait("tensor", bf[0], bf[1])
                for k in range(NG):
                    fn = lambda t, _k=k, _j=j: t.matmul(
                        ps[:, 3584 + _j * 16:3600 + _j * 16],
                        s_xn[:, _k * TA + WW + _j * 128:_k * TA + WW + (_j + 1) * 128],
                        s_wr[:, _k * 16:(_k + 1) * 16],
                        start=(_k == 0 and _j == 0), stop=(_k == NG - 1),
                        skip_group_check=(_j > 0))
                    if k == NG - 1:
                        t_rt = em.step("tensor", "m_t", fn)
                    else:
                        em.plain("tensor", fn)
            em.wait("vector", "m_t", t_rt)
            v_lg = em.step("vector", "m_v", lambda v: v.tensor_copy(
                s_lgall[:], ps[:, 3584:3584 + NCHM * 16]))
            bank_free[7] = ("m_v", v_lg)
            prev_m2 = None
            for j in range(NCHM):
                em.wait("scalar", "m_v", v_lg)
                if prev_m2 is not None:
                    em.wait("scalar", "m_v", prev_m2)
                s_exp = em.step("scalar", "m_s", lambda s, _j=j: s.activation(
                    s_pr[:], s_lgall[:, _j * 16:(_j + 1) * 16], AF.Exp))
                em.wait("vector", "m_s", s_exp)
                v_su = em.step("vector", "m_v", lambda v: v.reduce_sum(s_stat[:, 1:2], s_pr[:], axis=AX))
                em.wait("vector", "m_v", v_su)
                v_rc = em.step("vector", "m_v", lambda v: v.reciprocal(s_stat[:, 2:3], s_stat[:, 1:2]))
                em.wait("vector", "m_v", v_rc)
                v_pb = em.step("vector", "m_v", lambda v: v.tensor_scalar_mul(s_ms[:], s_pr[:], s_stat[:, 2:3]))
                em.wait("vector", "m_v", v_pb)
                v_m1 = em.step("vector", "m_v", lambda v: v.reduce_max(s_stat[:, 3:4], s_ms[:], axis=AX))
                em.wait("vector", "m_v", v_m1)
                v_eq = em.step("vector", "m_v", lambda v: v.tensor_single_scalar(
                    s_lg[:], s_ms[:], s_stat[:, 3:4], op=OP.is_ge))
                em.wait("vector", "m_v", v_eq)
                v_mk = em.step("vector", "m_v", lambda v: v.scalar_tensor_tensor(
                    s_pr[:], s_lg[:], -BIG, s_ms[:], op0=OP.mult, op1=OP.add))
                em.wait("vector", "m_v", v_mk)
                v_m2 = em.step("vector", "m_v", lambda v: v.reduce_max(s_stat[:, 4:5], s_pr[:], axis=AX))
                em.wait("vector", "m_v", v_m2)
                prev_m2 = v_m2
                v_kp = em.step("vector", "m_v", lambda v: v.tensor_single_scalar(
                    s_lg[:], s_ms[:], s_stat[:, 4:5], op=OP.is_ge))
                em.wait("vector", "m_v", v_kp)
                v_gate_done[j] = em.step("vector", "m_v", lambda v, _j=j: v.tensor_mul(
                    s_gate[:, _j * 16:(_j + 1) * 16], s_ms[:], s_lg[:]))




        qw_dn = {}
        t_up_last, t_dn_last = {}, {}
        prev_tile_tdn = None
        prev_gate_v = None
        dn_idx = 0
        v_rep_by_r, t_G_by_r, v_gate_last_by_r = {}, {}, {}
        qt_loaded = None
        qs_last = None
        qe_cnt = {}
        t_xg_last = {}
        s_xg_evac = {}

        def vstep(fn):
            vv = em.step("vector", "m_v", fn)
            em.wait("vector", "m_v", vv)
            return vv

        em.wait("vector", "m_t", t_pack[(order0[-1], 3)])
        v_z = em.step("vector", "m_v", lambda v: v.memset(s_do[:], 0.0))
        em.wait("sync", "m_v", v_z)
        qr = 0
        for rrow in range(0, 2 * RBROWS, 128):
            nr = min(128, 2 * RBROWS - rrow)
            qr = em.dma("sync", "m_qr", lambda s, _r=rrow, _n=nr: s.dma_start(
                h_rb[_r:_r + _n, :], s_do[:, 0:_n * 8]))
        qr_zero = qr

        def emit_tables():
            nonlocal qt_loaded, qr_zero
            vz2 = em.step("vector", "m_v", lambda v: v.memset(s_it[:], 0.0))
            em.wait("sync", "m_v", vz2)
            em.dma("sync", "m_qr", lambda s: s.dma_start(h_it[:, 0:1].rearrange(
                "(p a) o -> p (a o)", p=128), s_it[:]))
            qr_zero = em.dma("sync", "m_qr", lambda s: s.dma_start(
                h_gt2[:, 0:1].rearrange("(p a) o -> p (a o)", p=128), s_it[:]))
            em.wait("vector", "m_v", v_gate_done[NCHM - 1])
            for half in range(2):
                for q in range(4):
                    c = half * 4 + q
                    em.wait("tensor", "m_v", v_gate_done[c])
                    bf = bank_free.get(6)
                    if bf:
                        em.wait("tensor", bf[0], bf[1])
                    fn = lambda t, _q=q, _c=c: t.transpose(
                        ps[0:16, 3072 + _q * 128:3072 + (_q + 1) * 128],
                        s_gate[:, _c * 16:(_c + 1) * 16], s_eyef[:])
                    if q == 3:
                        t_gx = em.step("tensor", "m_t", fn)
                    else:
                        em.plain("tensor", fn)
                em.wait("scalar", "m_t", t_gx)
                sv = em.step("scalar", "m_s", lambda s, _h=half: s.activation(
                    s_gfm[0:16, _h * 512:(_h + 1) * 512], ps[0:16, 3072:3584], AF.Copy))
                bank_free[6] = ("m_s", sv)
            em.wait("vector", "m_s", sv)
            em.wait("vector", "m_qr", qr_zero)
            vstep(lambda v: v.tensor_single_scalar(
                s_do[0:16, 0:1024], s_gfm[0:16, 0:1024], 0.0, op=OP.is_gt))
            v_rk = vstep(lambda v: v.tensor_tensor_scan(
                s_gfm[0:16, 0:1024], s_do[0:16, 0:1024], s_do[0:16, 0:1024], 0.0,
                op0=OP.add, op1=OP.bypass))
            for c in range(8):
                em.wait("tensor", "m_v", v_rk)
                bf = bank_free.get(6)
                if bf:
                    em.wait("tensor", bf[0], bf[1])
                fn = lambda t, _c=c: t.transpose(
                    ps[:, 3072 + _c * 16:3072 + (_c + 1) * 16],
                    s_gfm[0:16, _c * 128:(_c + 1) * 128], s_eyef[0:16, 0:16])
                if c == 7:
                    t_rk = em.step("tensor", "m_t", fn)
                else:
                    em.plain("tensor", fn)
            em.wait("scalar", "m_t", t_rk)
            s_rv = em.step("scalar", "m_s", lambda s: s.activation(
                s_rkt[:, 0:128], ps[:, 3072:3200], AF.Copy))
            bank_free[6] = ("m_s", s_rv)
            em.wait("vector", "m_s", s_rv)
            qt = 0
            for c in range(NCHM):
                gate_c = s_gate[:, c * 16:(c + 1) * 16]
                rk_c = s_rkt[:, c * 16:(c + 1) * 16]
                if c >= 2:
                    for sname, qv in qt_by_c[c - 2]:
                        em.wait("vector", sname, qv)
                vstep(lambda v, _g=gate_c: v.tensor_single_scalar(
                    s_stat[:, 0:16], _g, 0.0, op=OP.is_gt))
                vstep(lambda v: v.tensor_tensor_scan(
                    s_stat[:, 16:32], s_stat[:, 0:16], s_stat[:, 0:16], 0.0,
                    op0=OP.add, op1=OP.bypass))
                vstep(lambda v: v.tensor_single_scalar(
                    s_lg[:], s_stat[:, 16:32], 1.0, op=OP.is_equal))
                vstep(lambda v: v.tensor_mul(s_ms[:], s_lg[:], s_stat[:, 0:16]))
                vstep(lambda v: v.tensor_single_scalar(
                    s_lg[:], s_stat[:, 16:32], 2.0, op=OP.is_equal))
                vstep(lambda v: v.tensor_mul(s_pr[:], s_lg[:], s_stat[:, 0:16]))
                vstep(lambda v, _r=rk_c: v.tensor_single_scalar(
                    s_stat[:, 32:48], _r, 129.0, op=OP.is_ge))
                vstep(lambda v, _r=rk_c: v.tensor_scalar_add(s_stat[:, 48:64], _r, -1.0))
                vstep(lambda v: v.scalar_tensor_tensor(
                    s_stat[:, 48:64], s_stat[:, 32:48], -128.0, s_stat[:, 48:64],
                    op0=OP.mult, op1=OP.add))
                vstep(lambda v: v.scalar_tensor_tensor(
                    s_stat[:, 0:16], s_stat[:, 48:64], 32.0, s_cst[:, 0:16],
                    op0=OP.mult, op1=OP.add))
                vstep(lambda v: v.tensor_add(s_stat[:, 16:32], s_stat[:, 0:16],
                                             s_stat[:, 32:48]))
                B2 = (c % 2) * 8
                vstep(lambda v: v.tensor_mul(s_lg[:], s_ms[:], s_stat[:, 16:32]))
                vstep(lambda v, _B=B2: v.reduce_sum(s_ti[:, _B:_B + 1], s_lg[:], axis=AX))
                vstep(lambda v: v.tensor_mul(s_lg[:], s_pr[:], s_stat[:, 16:32]))
                vstep(lambda v, _B=B2: v.reduce_sum(s_ti[:, _B + 1:_B + 2], s_lg[:], axis=AX))
                vstep(lambda v, _g=gate_c: v.tensor_mul(s_lg[:], s_ms[:], _g))
                vstep(lambda v, _B=B2: v.reduce_sum(s_ti[:, _B + 2:_B + 3], s_lg[:], axis=AX))
                vstep(lambda v, _g=gate_c: v.tensor_mul(s_lg[:], s_pr[:], _g))
                vstep(lambda v, _B=B2: v.reduce_sum(s_ti[:, _B + 3:_B + 4], s_lg[:], axis=AX))
                vstep(lambda v, _B=B2, _c=c: v.tensor_scalar(
                    s_ti[:, _B + 4:_B + 5], s_cst[:, 16:17], 1.0, float(_c * 128 + 1),
                    op0=OP.mult, op1=OP.add))
                cb = (c % 2) * 2
                vstep(lambda v, _B=B2, _cb=cb: v.tensor_copy(
                    s_u32[:, _cb:_cb + 1], s_ti[:, _B:_B + 1]))
                v_cst2 = vstep(lambda v, _B=B2, _cb=cb: v.tensor_copy(
                    s_u32[:, _cb + 1:_cb + 2], s_ti[:, _B + 1:_B + 2]))
                em.wait("gpsimd", "m_v", v_cst2)
                em.wait("gpsimd", "m_qr", qr_zero)
                from concourse.bass import IndirectOffsetOnAxis as IOA
                qt = []
                for si, (srcc, idxc) in enumerate(((4, 0), (4, 1), (2, 0), (3, 1))):
                    tgt = h_it if srcc == 4 else h_gt2
                    sname = "m_qt" if si % 2 == 0 else "m_qu"
                    qt.append((sname, em.dma("gpsimd", sname, lambda g, _s=srcc,
                              _i=idxc, _B=B2, _cb=cb, _t=tgt: g.indirect_dma_start(
                        _t, IOA(ap=s_u32[:, _cb + _i:_cb + _i + 1], axis=0),
                        s_ti[:, _B + _s:_B + _s + 1], None))))
                qt_by_c[c] = qt
            qt = em.dma("gpsimd", "m_qt", lambda g: g.dma_start(
                s_it[:], h_it[:, 0:1].rearrange("(p a) o -> p (a o)", p=128)))
            qt_loaded = em.dma("gpsimd", "m_qt", lambda g: g.dma_start(
                s_gv[:], h_gt2[:, 0:1].rearrange("(p a) o -> p (a o)", p=128)))

        def emit_reps(re):
            if re >= 1:
                em.wait("vector", "m_t", t_G_by_r[re - 1])
            em.wait("vector", "m_qt", qt_loaded)
            for st in range(NST):
                col = re * NST + st
                fn2 = lambda v, _st=st, _c=col: v.tensor_scalar_mul(
                    s_rep[:, _st * 128:(_st + 1) * 128], s_ones[:],
                    s_gv[:, _c:_c + 1])
                if st == NST - 1:
                    v_rep_by_r[re] = em.step("vector", "m_v", fn2)
                else:
                    em.plain("vector", fn2)

        def emit_idxprep(re):
            par = (re % 4) * 8
            tb = 8 + (re % 2) * 4
            em.wait("vector", "m_qt", qt_loaded)
            if re >= 4:
                if (re - 4, NST - 1) in qe_cnt:
                    em.wait("vector", *qe_cnt[(re - 4, NST - 1)])
                for _sp in qs_by_re.get(re - 4, ()):
                    em.wait("vector", *_sp)
            for st in range(NST):
                col = re * NST + st
                ti = s_it[:, col:col + 1]
                vstep(lambda v, _ti=ti, _tb=tb: v.tensor_scalar_add(
                    s_ti[:, _tb:_tb + 1], _ti, -1.0))
                vstep(lambda v, _tb=tb: v.tensor_scalar(
                    s_ti[:, _tb + 1:_tb + 2], s_ti[:, _tb:_tb + 1], 0.0, None,
                    op0=OP.max))
                vstep(lambda v, _p=par, _tb=tb, _st=st: v.tensor_copy(
                    s_u32[:, _p + _st:_p + _st + 1], s_ti[:, _tb + 1:_tb + 2]))
                vstep(lambda v, _ti=ti, _tb=tb: v.tensor_single_scalar(
                    s_ti[:, _tb + 1:_tb + 2], _ti, 0.0, op=OP.is_equal))
                vstep(lambda v, _tb=tb: v.tensor_sub(
                    s_ti[:, _tb + 2:_tb + 3], s_cst[:, 17:18], s_ti[:, _tb:_tb + 1]))
                vstep(lambda v, _tb=tb: v.tensor_mul(
                    s_ti[:, _tb + 2:_tb + 3], s_ti[:, _tb + 1:_tb + 2],
                    s_ti[:, _tb + 2:_tb + 3]))
                vstep(lambda v, _tb=tb: v.tensor_add(
                    s_ti[:, _tb + 3:_tb + 4], s_ti[:, _tb:_tb + 1],
                    s_ti[:, _tb + 2:_tb + 3]))
                vstep(lambda v, _p=par, _tb=tb, _st=st: v.tensor_copy(
                    s_u32[:, _p + 2 + _st:_p + 3 + _st], s_ti[:, _tb + 3:_tb + 4]))
                vstep(lambda v, _tb=tb: v.tensor_scalar_add(
                    s_ti[:, _tb + 3:_tb + 4], s_ti[:, _tb + 3:_tb + 4], float(RBROWS)))
                vstep(lambda v, _p=par, _tb=tb, _st=st: v.tensor_copy(
                    s_u32[:, _p + 4 + _st:_p + 5 + _st], s_ti[:, _tb + 3:_tb + 4]))

        def emit_gather_dma(re):
            from concourse.bass import IndirectOffsetOnAxis as IOA
            par = (re % 4) * 8
            em.wait("gpsimd", "m_v", v_idx_done[re])
            em.wait("gpsimd", "m_qh", wb_all)
            for st in range(NST):
                if re > 0:
                    em.wait("gpsimd", "m_t", t_xgp_by.get((re - 1, st)))
                elif st == 1:
                    em.wait("gpsimd", "m_t", t_dn_last[1])
                sn = "m_qe" if st == 0 else "m_qu"
                qe_cnt[(re, st)] = (sn, em.dma("gpsimd", sn, lambda g, _p=par, _st=st:
                                               g.indirect_dma_start(
                    xgt_ap(_st), None, h_xnd,
                    IOA(ap=s_u32[:, _p + _st:_p + _st + 1], axis=0))))

        def emit_gather_tp(re):
            for st in range(NST):
                for p in range(4):
                    bf = bank_free.get(p)
                    if bf:
                        em.wait("tensor", bf[0], bf[1])
                    em.wait("tensor", *qe_cnt[(re, st)])
                    for q in range(4):
                        g2 = 4 * p + q
                        fn = lambda t, _b=p, _q=q, _g=g2, _st=st: t.matmul(
                            ps[:, _b * 512 + _q * 128:_b * 512 + (_q + 1) * 128],
                            xgt_ap(_st)[:, _g * 128:(_g + 1) * 128],
                            s_eyeb[:], start=True, stop=True)
                        if q == 3:
                            t_xgp = em.step("tensor", "m_t", fn)
                        else:
                            em.plain("tensor", fn)
                    em.wait("scalar", "m_t", t_xgp)
                    em.wait("scalar", "m_t", t_up_last.get(re + NSH - 1))
                    sv2 = em.step("scalar", "m_s", lambda s, _b=p, _p2=p, _st=st: s.activation(
                        s_xg[:].rearrange("p (k s) -> p k s", s=CAP)[
                            :, 4 * _p2:4 * _p2 + 4, _st * 128:(_st + 1) * 128],
                        ps[:, _b * 512:_b * 512 + 512].rearrange("p (g t) -> p g t", t=128),
                        AF.Copy))
                    bank_free[p] = ("m_s", sv2)
                    s_xg_evac[re] = sv2
                t_xgp_by[(re, st)] = t_xgp

        v_idx_done = {}
        t_xgp_by = {}
        qt_by_c = {}
        qs_by_re = {}
        wb_all = max(wb_by_idx.values())

        em.wait("gpsimd", "m_v", v_norm[order0[-1]])
        q0dn = em.dma("gpsimd", "m_qb",
                      lambda g: g.dma_start(s_wb[:], h_dn[0], max_dma_last_dim=8192))
        for _sl in range(4):
            qw_dn[(0, _sl)] = ("m_qb", q0dn)

        for e in range(NSH):
            for tt in range(2):
                if e == 0 and tt == 1:
                    emit_router()
                if e == 1 and tt == 0:
                    emit_tables()
                    emit_reps(0)
                    v_idx_done[0] = None
                    emit_idxprep(0)
                    v_idx_done[0] = em.cnt["m_v"]
                toff = WW + tt * 512
                s_gelu = {}
                em.wait("tensor", "m_s", v_evac[(5 if tt == 0 else NCH0 - 1, 3)])
                em.wait("tensor", "m_qa", qw_up[e])
                for m in range(NMG):
                    b = m % 4
                    bf = bank_free.get(b)
                    if bf:
                        em.wait("tensor", bf[0], bf[1])
                    for k in range(NG):
                        fn = lambda t, _b=b, _m=m, _k=k, _to=toff: t.matmul(
                            ps[:, _b * 512:(_b + 1) * 512],
                            s_wa[:, _k * DM + _m * 128:_k * DM + (_m + 1) * 128],
                            s_xn[:, _k * TA + _to:_k * TA + _to + 512],
                            start=(_k == 0), stop=(_k == NG - 1))
                        if k == NG - 1:
                            t_upg = em.step("tensor", "m_t", fn)
                        else:
                            em.plain("tensor", fn)
                    if m == NMG - 1:
                        t_up_last[e] = t_upg
                    em.wait("scalar", "m_t", t_upg)
                    if prev_tile_tdn is not None:
                        em.wait("scalar", "m_t", prev_tile_tdn)
                    s_gelu[m] = em.step("scalar", "m_s", lambda s, _b=b, _m=m: s.activation(
                        s_gt[:, _m * 512:(_m + 1) * 512],
                        ps[:, _b * 512:(_b + 1) * 512], AF.Gelu))
                    bank_free[b] = ("m_s", s_gelu[m])
                if tt == 1:
                    em.wait("gpsimd", "m_t", t_up_last[e])
                    qw_up[e + 1] = em.dma("gpsimd", "m_qa", lambda g, _e=e + 1:
                                          g.dma_start(s_wa[:], h_up[_e],
                                                      max_dma_last_dim=8192))
                for hg in range(NG):
                    if hg % 4 == 0:
                        em.wait("tensor", *qw_dn[(e, hg // 4)])
                    b = 4 + (dn_idx % 2)
                    dn_idx += 1
                    bf = bank_free.get(b)
                    if bf:
                        em.wait("tensor", bf[0], bf[1])
                    for k in range(NMG):
                        em.wait("tensor", "m_s", s_gelu[k])
                        fn = lambda t, _b=b, _hg=hg, _k=k: t.matmul(
                            ps[:, _b * 512:(_b + 1) * 512],
                            s_wb[:, _k * D + _hg * 128:_k * D + (_hg + 1) * 128],
                            s_gt[:, _k * 512:(_k + 1) * 512],
                            start=(_k == 0), stop=(_k == NMG - 1))
                        if k == NMG - 1:
                            t_dng = em.step("tensor", "m_t", fn)
                        else:
                            em.plain("tensor", fn)
                    em.wait("vector", "m_t", t_dng)
                    aslice = (hg * TM + tt * 512, hg * TM + tt * 512 + 512)
                    if e == 0:
                        v_add = em.step("vector", "m_v", lambda v, _b=b, _a=aslice: v.tensor_copy(
                            s_acc[:, _a[0]:_a[1]], ps[:, _b * 512:(_b + 1) * 512]))
                    else:
                        v_add = em.step("vector", "m_v", lambda v, _b=b, _a=aslice: v.tensor_tensor(
                            s_acc[:, _a[0]:_a[1]], s_acc[:, _a[0]:_a[1]],
                            ps[:, _b * 512:(_b + 1) * 512], op=OP.add))
                    bank_free[b] = ("m_v", v_add)
                    if tt == 1 and hg % 4 == 3:
                        sl = hg // 4
                        sn = "m_qb" if sl % 2 == 0 else "m_qv"
                        em.wait("gpsimd", "m_t", t_dng)
                        qw_dn[(e + 1, sl)] = (sn, em.dma(
                            "gpsimd", sn, lambda g, _e=e + 1, _sl=sl: g.dma_start(
                                s_wb[:].rearrange("p (k d) -> p k d", k=NMG)[
                                    :, :, _sl * 512:(_sl + 1) * 512],
                                h_dn[_e][:, :, _sl * 512:(_sl + 1) * 512])))
                t_dn_last[e] = t_dng
                prev_tile_tdn = t_dng
                v_dn_final = v_add

        prev_qsc = {}
        emit_gather_dma(0)
        emit_gather_tp(0)
        for _q in range(4):
            qw_up[(NSH, _q)] = ("m_qa", qw_up[NSH])

        from concourse.bass import IndirectOffsetOnAxis as IOA2
        for re in range(NRT):
            e = re + NSH
            gb = 6 + (re % 2)
            par = (re % 4) * 8
            em.wait("tensor", "m_v", v_rep_by_r[re])
            bfg = bank_free.get(gb)
            if bfg:
                em.wait("tensor", bfg[0], bfg[1])
            if re >= 2 and (re - 2) in v_gate_last_by_r:
                em.wait("tensor", "m_v", v_gate_last_by_r[re - 2])
            for st in range(NST):
                fn = lambda t, _st=st, _gb=gb: t.transpose(
                    ps[:, _gb * 512 + _st * 128:_gb * 512 + (_st + 1) * 128],
                    s_rep[:, _st * 128:(_st + 1) * 128], s_eyef[:])
                if st == NST - 1:
                    t_G_by_r[re] = em.step("tensor", "m_t", fn)
                else:
                    em.plain("tensor", fn)
            if re + 1 < NRT:
                emit_reps(re + 1)
                emit_idxprep(re + 1)
                v_idx_done[re + 1] = em.cnt["m_v"]
                emit_gather_dma(re + 1)

            v_gate = {}
            em.wait("tensor", "m_s", s_xg_evac[re])
            for m in (6, 7):
                b = m - 2
                bf = bank_free.get(b)
                if bf:
                    em.wait("tensor", bf[0], bf[1])
                for k in range(NG):
                    if m == 6:
                        em.wait("tensor", *qw_up[(e, k // 4)])
                    fn = lambda t, _b=b, _m=m, _k=k: t.matmul(
                        ps[:, _b * 512:_b * 512 + CAP],
                        s_wa[:, _k * DM + _m * 128:_k * DM + (_m + 1) * 128],
                        s_xg[:, _k * CAP:(_k + 1) * CAP],
                        start=(_k == 0), stop=(_k == NG - 1))
                    if k == NG - 1:
                        t_upg = em.step("tensor", "m_t", fn)
                    else:
                        em.plain("tensor", fn)
                em.wait("scalar", "m_t", t_upg)
                em.wait("scalar", "m_t", prev_tile_tdn)
                if prev_gate_v is not None:
                    em.wait("scalar", "m_v", prev_gate_v)
                if "j1" in prev_qsc:
                    em.wait("scalar", "m_qi", prev_qsc["j1"])
                s_gelu_m = em.step("scalar", "m_s", lambda s, _b=b, _m=m: s.activation(
                    s_mid[:, _m * CAP:(_m + 1) * CAP],
                    ps[:, _b * 512:_b * 512 + CAP], AF.Gelu))
                bank_free[b] = ("m_s", s_gelu_m)
                em.wait("vector", "m_s", s_gelu_m)
                v_gate[m] = em.step("vector", "m_v", lambda v, _m=m, _gb=gb: v.tensor_tensor(
                    s_gt[:, _m * CAP:(_m + 1) * CAP],
                    s_mid[:, _m * CAP:(_m + 1) * CAP],
                    ps[:, _gb * 512:_gb * 512 + CAP], op=OP.mult))
            for b6 in range(6):
                bf = bank_free.get(b6)
                if bf:
                    em.wait("tensor", bf[0], bf[1])
            t_q = {}
            for k in range(NG):
                for m in range(6):
                    fn = lambda t, _m=m, _k=k: t.matmul(
                        ps[:, _m * 512:_m * 512 + CAP],
                        s_wa[:, _k * DM + _m * 128:_k * DM + (_m + 1) * 128],
                        s_xg[:, _k * CAP:(_k + 1) * CAP],
                        start=(_k == 0), stop=(_k == NG - 1))
                    if m == 5:
                        t_q[k] = em.step("tensor", "m_t", fn)
                    else:
                        em.plain("tensor", fn)
                if k % 4 == 3 and e + 1 < NEXP:
                    qq = k // 4
                    sn = "m_qa" if qq % 2 == 0 else "m_qw"
                    em.wait("gpsimd", "m_t", t_q[k])
                    qw_up[(e + 1, qq)] = (sn, em.dma(
                        "gpsimd", sn, lambda g, _e=e + 1, _q=qq: g.dma_start(
                            s_wa[:, _q * 4 * DM:(_q + 1) * 4 * DM],
                            h_up[_e][:, _q * 4:(_q + 1) * 4, :],
                            max_dma_last_dim=8192)))
            t_up_last[e] = t_q[NG - 1]
            for m in range(6):
                em.wait("scalar", "m_t", t_q[NG - 1])
                em.wait("scalar", "m_t", prev_tile_tdn)
                if prev_gate_v is not None:
                    em.wait("scalar", "m_v", prev_gate_v)
                if m >= 4 and "j1" in prev_qsc:
                    em.wait("scalar", "m_qi", prev_qsc["j1"])
                if m < 4 and "j3" in prev_qsc:
                    em.wait("scalar", "m_qs", prev_qsc["j3"])
                s_gelu_m = em.step("scalar", "m_s", lambda s, _m=m: s.activation(
                    s_mid[:, _m * CAP:(_m + 1) * CAP],
                    ps[:, _m * 512:_m * 512 + CAP], AF.Gelu))
                bank_free[m] = ("m_s", s_gelu_m)
                em.wait("vector", "m_s", s_gelu_m)
                v_gate[m] = em.step("vector", "m_v", lambda v, _m=m, _gb=gb: v.tensor_tensor(
                    s_gt[:, _m * CAP:(_m + 1) * CAP],
                    s_mid[:, _m * CAP:(_m + 1) * CAP],
                    ps[:, _gb * 512:_gb * 512 + CAP], op=OP.mult))
            prev_gate_v = v_gate[5]
            v_gate_last_by_r[re] = v_gate[5]

            if re + 1 < NRT:
                emit_gather_tp(re + 1)

            PB = [s_mid[:, 0:1024], s_mid[:, 1024:2048], s_do[:, 0:1024],
                  s_mid[:, 0:1024]]
            SM = ["m_qs", "m_qi", "m_qj", "m_qs"]
            qs_j = {}
            for st in range(NST):
                for sl in range(4):
                    jj = st * 2 + sl // 2
                    em.wait("tensor", *qw_dn[(e, sl)])
                    b = 4 + (dn_idx % 2)
                    dn_idx += 1
                    bf = bank_free.get(b)
                    if bf:
                        em.wait("tensor", bf[0], bf[1])
                    for k in range(NMG):
                        em.wait("tensor", "m_v", v_gate[k])
                        fn = lambda t, _b=b, _st=st, _sl=sl, _k=k: t.matmul(
                            ps[:, _b * 512:(_b + 1) * 512],
                            s_gt[:, _k * CAP + _st * 128:_k * CAP + (_st + 1) * 128],
                            s_wb[:, _k * D + _sl * 512:_k * D + (_sl + 1) * 512],
                            start=(_k == 0), stop=(_k == NMG - 1))
                        if k == NMG - 1:
                            t_dng = em.step("tensor", "m_t", fn)
                        else:
                            em.plain("tensor", fn)
                    em.wait("vector", "m_t", t_dng)
                    if sl % 2 == 0:
                        if jj == 0 and "j3" in prev_qsc:
                            em.wait("vector", "m_qs", prev_qsc["j3"])
                        elif jj == 1 and "j1" in prev_qsc:
                            em.wait("vector", "m_qi", prev_qsc["j1"])
                        elif jj == 2 and "j2" in prev_qsc:
                            em.wait("vector", "m_qj", prev_qsc["j2"])
                        elif jj == 3:
                            em.wait("vector", "m_qs", qs_j[0])
                    v_ev = em.step("vector", "m_v", lambda v, _b=b, _sl=sl,
                                   _pb=PB[jj]: v.tensor_copy(
                        _pb[:, (_sl % 2) * 512:((_sl % 2) + 1) * 512],
                        ps[:, _b * 512:(_b + 1) * 512]))
                    bank_free[b] = ("m_v", v_ev)
                    if st == NST - 1 and e + 1 < NEXP:
                        sn = "m_qb" if sl % 2 == 0 else "m_qv"
                        em.wait("gpsimd", "m_t", t_dng)
                        qw_dn[(e + 1, sl)] = (sn, em.dma(
                            "gpsimd", sn, lambda g, _e=e + 1, _sl=sl: g.dma_start(
                                s_wb[:].rearrange("p (k d) -> p k d", k=NMG)[
                                    :, :, _sl * 512:(_sl + 1) * 512],
                                h_dn[_e][:, :, _sl * 512:(_sl + 1) * 512])))
                    if sl % 2 == 1:
                        hh = sl // 2
                        em.wait("gpsimd", "m_v", v_ev)
                        em.wait("gpsimd", "m_qr", qr_zero)
                        ic = par + 2 + st + 2 * hh
                        qs_j[jj] = em.dma("gpsimd", SM[jj], lambda g, _ic=ic,
                                          _pb=PB[jj]: g.indirect_dma_start(
                            h_rb, IOA2(ap=s_u32[:, _ic:_ic + 1], axis=0),
                            _pb, None, compute_op=OP.add, oob_is_err=True))
            prev_qsc = {"j1": qs_j[1], "j2": qs_j[2], "j3": qs_j[3]}
            qs_by_re[re] = [("m_qi", qs_j[1]), ("m_qj", qs_j[2]),
                            ("m_qs", qs_j[3])]
            qs_last = qs_by_re[re]
            t_dn_last[e] = t_dng
            prev_tile_tdn = t_dng

        v_merge = {}
        for j in range(NCHM):
            for _sp in qs_last:
                em.wait("sync", *_sp)
            if j >= 2:
                em.wait("sync", "m_v", v_merge[(j - 2, 3)])
            for hh in range(2):
                qm = em.dma("sync", "m_qr", lambda s, _j=j, _h=hh: s.dma_start(
                    s_ot[:].rearrange("p (s h c) -> p s h c", s=2, h=2)[
                        :, _j % 2, _h, :],
                    h_rb[_h * RBROWS + _j * 128:_h * RBROWS + (_j + 1) * 128, :]))
            for p in range(4):
                b = p
                em.wait("tensor", "m_qr", qm)
                bf = bank_free.get(b)
                if bf:
                    em.wait("tensor", bf[0], bf[1])
                for q in range(4):
                    g2 = 4 * p + q
                    fn = lambda t, _b=b, _q=q, _g=g2, _j=j: t.transpose(
                        ps[:, _b * 512 + _q * 128:_b * 512 + (_q + 1) * 128],
                        s_ot[:, (_j % 2) * D + _g * 128:(_j % 2) * D + (_g + 1) * 128],
                        s_eyef[:])
                    if q == 3:
                        t_mg = em.step("tensor", "m_t", fn)
                    else:
                        em.plain("tensor", fn)
                em.wait("vector", "m_t", t_mg)
                v_merge[(j, p)] = em.step("vector", "m_v", lambda v, _b=b, _p=p, _j=j: v.tensor_tensor(
                    s_acc[:].rearrange("p (g t) -> p g t", t=TM)[:, 4 * _p:4 * _p + 4,
                                                                _j * 128:(_j + 1) * 128],
                    s_acc[:].rearrange("p (g t) -> p g t", t=TM)[:, 4 * _p:4 * _p + 4,
                                                                _j * 128:(_j + 1) * 128],
                    ps[:, _b * 512:_b * 512 + 512].rearrange("p (g t) -> p g t", t=128),
                    op=OP.add))
                bank_free[b] = ("m_v", v_merge[(j, p)])
        v_merge_last = v_merge[(NCHM - 1, 3)]

        if DEBUG_DUMPS:
            em.wait("sync", "m_v", v_gate_done[NCHM - 1])
            em.dma("sync", "m_qd", lambda s: s.dma_start(h_d_gate, s_gate[:]))
            em.dma("sync", "m_qd", lambda s: s.dma_start(h_d_lg, s_lgall[:]))

        d_acc1 = None
        if DEBUG_DUMPS:
            em.wait("sync", "m_v", v_dn_final)
            d_acc1 = em.dma("sync", "m_qd", lambda s: s.dma_start(h_d_acc1, s_acc[:]))

        p4slot = [s_xt[:, 0:D], s_xt[:, D:2 * D], s_xnf[:, 0:D],
                  s_xnf[:, D:2 * D], s_xnf3[:, 0:D],
                  s_xt[:, 0:D], s_xt[:, D:2 * D], s_xnf[:, 0:D]]
        qx4, qsem4 = {}, {}

        qw_mz, qw_mn = {}, {}
        t_k_last, t_h_last = {}, {}
        s_az, s_s12, s_sig = {}, {}, {}
        v_b, v_sg = {}, {}
        tiles3 = [(0, 512, 0), (512, 512, 1), (1024, 256, 2)]

        for g in range(NG):
            if g < 2:
                em.wait("gpsimd", "m_t", t_dn_last[NEXP - 1])
            else:
                em.wait("gpsimd", "m_t", t_k_last[g - 2])
            qw_mz[g] = em.dma("gpsimd", "m_qz", lambda gp, _g=g: gp.dma_start(
                s_mz[:, (_g % 2) * 2048:(_g % 2) * 2048 + 2048], h_wz[_g],
                max_dma_last_dim=8192))
            if g >= 2:
                em.wait("gpsimd", "m_t", t_h_last[g - 2])
            qw_mn[g] = em.dma("gpsimd", "m_qn", lambda gp, _g=g: gp.dma_start(
                s_mn[:, (_g % 2) * 2048:(_g % 2) * 2048 + 2048], h_wn[_g],
                max_dma_last_dim=8192))
            if g == 8:
                for _sp in qs_last:
                    em.wait("gpsimd", *_sp)
                for j4 in range(5):
                    qsem4[j4] = "m_qg"
                    qx4[j4] = em.dma("gpsimd", "m_qg", lambda gp, _j=j4: gp.dma_start(
                        p4slot[_j], h_x[WW + _j * 128:WW + (_j + 1) * 128, :]))

            if g == 0:
                em.wait("tensor", "m_s", v_evac[(order0[-1], 3)])
            em.wait("tensor", "m_qz", qw_mz[g])
            if g >= 1:
                em.wait("tensor", "m_s", s_az[g - 1])
            else:
                for b in range(3):
                    bf = bank_free.get(b)
                    if bf:
                        em.wait("tensor", bf[0], bf[1])
            for off, w, b in tiles3:
                for k in range(NG):
                    fn = lambda t, _b=b, _o=off, _w=w, _k=k, _g=g: t.matmul(
                        ps[:, _b * 512:_b * 512 + _w],
                        s_mz[:, (_g % 2) * 2048 + _k * 128:(_g % 2) * 2048 + (_k + 1) * 128],
                        s_xn[:, _k * TA + _o:_k * TA + _o + _w],
                        start=(_k == 0), stop=(_k == NG - 1))
                    if k == NG - 1:
                        t_kt = em.step("tensor", "m_t", fn)
                    else:
                        em.plain("tensor", fn)
            t_k_last[g] = t_kt
            em.wait("tensor", "m_qn", qw_mn[g])
            if g >= 1:
                em.wait("tensor", "m_s", s_s12[g - 1])
            else:
                for b in range(3, 6):
                    bf = bank_free.get(b)
                    if bf:
                        em.wait("tensor", bf[0], bf[1])
            for off, w, b in tiles3:
                for k in range(NG):
                    fn = lambda t, _b=b, _o=off, _w=w, _k=k, _g=g: t.matmul(
                        ps[:, 1536 + _b * 512:1536 + _b * 512 + _w],
                        s_mn[:, (_g % 2) * 2048 + _k * 128:(_g % 2) * 2048 + (_k + 1) * 128],
                        s_xn[:, _k * TA + _o:_k * TA + _o + _w],
                        start=(_k == 0), stop=(_k == NG - 1))
                    if k == NG - 1:
                        t_ht = em.step("tensor", "m_t", fn)
                    else:
                        em.plain("tensor", fn)
            t_h_last[g] = t_ht

            if g >= 1:
                em.wait("scalar", "m_v", v_sg[g - 1])
            em.wait("scalar", "m_t", t_k_last[g])
            em.step("scalar", "m_s", lambda s, _g=g: s.activation(
                s_ka[:], ps[:, 0:TA], AF.Sigmoid, bias=s_nbz[:, _g:_g + 1], scale=-1.0))
            s_az[g] = em.step("scalar", "m_s", lambda s, _g=g: s.activation(
                s_kz[:], ps[:, 0:TA], AF.Sigmoid, bias=s_bz[:, _g:_g + 1]))
            em.wait("scalar", "m_t", t_h_last[g])
            em.step("scalar", "m_s", lambda s, _g=g: s.activation(
                s_s1[:], ps[:, 1536:1536 + TA], AF.Sigmoid, bias=s_bn[:, _g:_g + 1]))
            s_s12[g] = em.step("scalar", "m_s", lambda s, _g=g: s.activation(
                s_s2[:], ps[:, 1536:1536 + TA], AF.Identity, bias=s_bnp5[:, _g:_g + 1]))

            em.wait("vector", "m_s", s_s12[g])
            em.plain("vector", lambda v: v.tensor_max(s_sh[:], s_s1[:], s_s2[:]))
            v_b[g] = em.step("vector", "m_v", lambda v: v.tensor_mul(s_s1[:], s_kz[:], s_sh[:]))
            v_sc1 = em.step("vector", "m_v", lambda v: v.tensor_tensor_scan(
                s_s2[:, 0:WW], s_ka[:, 0:WW], s_s1[:, 0:WW], 0.5,
                op0=OP.mult, op1=OP.add))
            em.wait("vector", "m_v", v_sc1)
            v_t1 = em.step("vector", "m_v", lambda v: v.tensor_scalar(
                s_stat[:, 0:1], s_s2[:, WW - 1:WW], -1.0, 0.5, op0=OP.mult, op1=OP.add))
            em.wait("vector", "m_v", v_t1)
            v_t2 = em.step("vector", "m_v", lambda v: v.tensor_mul(
                s_stat[:, 1:2], s_stat[:, 0:1], s_flag[:]))
            em.wait("vector", "m_v", v_t2)
            v_ini = em.step("vector", "m_v", lambda v: v.tensor_add(
                s_i[:], s_s2[:, WW - 1:WW], s_stat[:, 1:2]))
            em.wait("vector", "m_v", v_ini)
            v_scan2 = em.step("vector", "m_v", lambda v: v.tensor_tensor_scan(
                s_s2[:, WW:TA], s_ka[:, WW:TA], s_s1[:, WW:TA], s_i[:, 0:1],
                op0=OP.mult, op1=OP.add))
            em.wait("scalar", "m_v", v_b[g])
            em.wait("scalar", "m_v", v_merge_last)
            s_sig[g] = em.step("scalar", "m_s", lambda s, _g=g: s.activation(
                s_kz[:, 0:TM], s_acc[:, _g * TM:(_g + 1) * TM], AF.Sigmoid))
            em.wait("vector", "m_s", s_sig[g])
            em.plain("vector", lambda v, _g=g: v.tensor_mul(
                s_ka[:, 0:TM], s_acc[:, _g * TM:(_g + 1) * TM], s_kz[:, 0:TM]))
            if DEBUG_DUMPS and g == 0:
                em.wait("vector", "m_qd", d_acc1)
            v_sg[g] = em.step("vector", "m_v", lambda v, _g=g: v.tensor_mul(
                s_acc[:, _g * TM:(_g + 1) * TM], s_s2[:, WW:TA], s_ka[:, 0:TM]))

        for _b in range(3):
            bank_free[_b] = ("m_s", s_az[NG - 1])
        for _b in range(3, 6):
            bank_free[_b] = ("m_s", s_s12[NG - 1])
        bank_free[6] = ("m_v", v_gate_last_by_r[NRT - 2])
        bank_free[7] = ("m_v", v_gate_last_by_r[NRT - 1])

        if DEBUG_DUMPS:
            em.wait("sync", "m_v", v_sg[NG - 1])
            em.dma("sync", "m_qd", lambda s: s.dma_start(h_d_sg, s_acc[:]))

        v_fadd = {}
        qo, oq = {}, {}
        def emit_p4_load(j):
            eng4, sem4 = (("sync", "m_qx") if j % 2 == 0 else ("scalar", "m_qy"))
            qsem4[j] = sem4
            em.wait(eng4, "m_v", v_fadd[(j - 5, 3)])
            qx4[j] = em.dma(eng4, sem4, lambda s, _j=j: s.dma_start(
                p4slot[_j], h_x[WW + _j * 128:WW + (_j + 1) * 128, :]))
        P4B = (6, 7, 4, 5)
        for j in range(NCHM):
            jb = (j % 2) * D
            for p in range(4):
                b = P4B[p]
                em.wait("tensor", "m_v", v_sg[4 * p + 3])
                bf = bank_free.get(b)
                if bf:
                    em.wait("tensor", bf[0], bf[1])
                for q in range(4):
                    g = 4 * p + q
                    fn = lambda t, _b=b, _q=q, _g=g, _j=j: t.transpose(
                        ps[:, _b * 512 + _q * 128:_b * 512 + (_q + 1) * 128],
                        s_acc[:, _g * TM + _j * 128:_g * TM + (_j + 1) * 128], s_eyef[:])
                    if q == 3:
                        t_p4 = em.step("tensor", "m_t", fn)
                    else:
                        em.plain("tensor", fn)
                em.wait("vector", "m_t", t_p4)
                em.wait("vector", qsem4[j], qx4[j])
                if j >= 2:
                    em.wait("vector", oq[j - 2], qo[j - 2])
                v_fadd[(j, p)] = em.step("vector", "m_v", lambda v, _b=b, _p=p, _j=j, _jb=jb: v.tensor_tensor(
                    s_ot[:, _jb + _p * 512:_jb + (_p + 1) * 512],
                    ps[:, _b * 512:(_b + 1) * 512],
                    p4slot[_j][:, _p * 512:(_p + 1) * 512], op=OP.add))
                bank_free[b] = ("m_v", v_fadd[(j, p)])
            oeng, osem = (("sync", "m_qo") if j % 2 == 0 else ("gpsimd", "m_qp"))
            oq[j] = osem
            em.wait(oeng, "m_v", v_fadd[(j, 3)])
            qo[j] = em.dma(oeng, osem, lambda s, _j=j, _jb=jb: s.dma_start(
                h_out[_j * 128:(_j + 1) * 128, :], s_ot[:, _jb:_jb + D]))
            if j + 5 < NCHM:
                emit_p4_load(j + 5)
        em.wait("sync", "m_qo", qo[NCHM - 2])
        em.wait("sync", "m_qp", qo[NCHM - 1])

        block = stk.enter_context(nc.Block())

        @block.tensor
        def _(t):
            for f in em.ops["tensor"]:
                f(t, sems)

        @block.vector
        def _(v):
            for f in em.ops["vector"]:
                f(v, sems)

        @block.scalar
        def _(s):
            for f in em.ops["scalar"]:
                f(s, sems)

        @block.sync
        def _(s):
            for f in em.ops["sync"]:
                f(s, sems)

        @block.gpsimd
        def _(g):
            for f in em.ops["gpsimd"]:
                f(g, sems)

    return nc


def kernel(L0_attn_x, L0_attn_Wz, L0_attn_bz, L0_attn_Wn, L0_attn_bn,
           L0_ffn_Wsh_up, L0_ffn_Wsh_down, L0_ffn_Wrt_up, L0_ffn_Wrt_down,
           L0_ffn_W_router):
    x = np.ascontiguousarray(np.asarray(L0_attn_x, dtype=np.float32).reshape(8192, D))
    Wz = np.asarray(L0_attn_Wz, dtype=np.float32)
    Wn = np.asarray(L0_attn_Wn, dtype=np.float32)
    bz = np.asarray(L0_attn_bz, dtype=np.float32)
    bn = np.asarray(L0_attn_bn, dtype=np.float32)
    Wr = np.asarray(L0_ffn_W_router, dtype=np.float32)
    up_all = np.concatenate([np.asarray(L0_ffn_Wsh_up, dtype=np.float32),
                             np.asarray(L0_ffn_Wrt_up, dtype=np.float32)], axis=0)
    dn_all = np.concatenate([np.asarray(L0_ffn_Wsh_down, dtype=np.float32),
                             np.asarray(L0_ffn_Wrt_down, dtype=np.float32)], axis=0)

    h_wz = np.ascontiguousarray(Wz.reshape(NG, 128, NG, 128).transpose(0, 3, 2, 1))
    h_wn = np.ascontiguousarray(Wn.reshape(NG, 128, NG, 128).transpose(0, 3, 2, 1))
    h_up = np.ascontiguousarray(up_all.reshape(NEXP, NG, 128, DM).transpose(0, 2, 1, 3))
    h_dn = np.ascontiguousarray(dn_all.reshape(NEXP, NMG, 128, D).transpose(0, 2, 1, 3))
    h_wr = np.ascontiguousarray(Wr.reshape(NG, 128, 16).transpose(1, 0, 2))
    h_bz = np.ascontiguousarray(bz.reshape(NG, 128).T)
    h_bn = np.ascontiguousarray(bn.reshape(NG, 128).T)
    h_eyef = np.eye(128, dtype=np.float32)
    h_cstv = np.zeros((128, 32), dtype=np.float32)
    h_cstv[:, 0:16] = 2.0 * np.arange(16, dtype=np.float32)[None, :]
    h_cstv[:, 16] = np.arange(128, dtype=np.float32)
    h_cstv[:, 17] = np.arange(128, dtype=np.float32) + TM

    nc = build_nc()
    in_maps = []
    for c in range(NCORES):
        if c == 0:
            xs = np.ascontiguousarray(np.concatenate([x[0:WW], x[0:TM]], axis=0))
            flag = np.ones((128, 1), dtype=np.float32)
        else:
            xs = np.ascontiguousarray(x[c * TM - WW:c * TM + TM])
            flag = np.zeros((128, 1), dtype=np.float32)
        in_maps.append({
            "h_x": xs, "h_up": h_up, "h_dn": h_dn, "h_wz": h_wz, "h_wn": h_wn,
            "h_wr": h_wr, "h_bz": h_bz, "h_bn": h_bn, "h_eyef": h_eyef,
            "h_flag": flag, "h_cst": h_cstv,
        })
    res = run_bass_kernel_spmd(nc, in_maps, core_ids=list(range(NCORES)))
    out = np.concatenate([np.asarray(res.results[c]["h_out"]) for c in range(NCORES)],
                         axis=0)
    return out.reshape(1, 8192, D).astype(np.float32, copy=False)
